# version 07
# baseline (speedup 1.0000x reference)
import numpy as np
import ml_dtypes
import contextlib
import concourse.bass as bass
import concourse.mybir as mybir
from concourse.bass_utils import run_bass_kernel_spmd

F32 = mybir.dt.float32
BF16 = mybir.dt.bfloat16
U16 = mybir.dt.uint16
AF = mybir.ActivationFunctionType
AL = mybir.AluOpType
AX = mybir.AxisListType

H = 16
D = 2048
TPC = 2048
TT = 256
NT_ATT = TPC // TT
ET = 512
NT_EPI = TPC // ET
HDIM = 8192
GS = 64
CSCALE = float(128) ** -0.5
EPS_LN = 1e-5
EPS_RMS = 2.220446049250313e-16

ENGS = ("tensor", "vector", "scalar", "sync", "gpsimd")


NLANE = 8
DMAQ = ("sync", "gpsimd", "scalar")
SEMNAMES = list(ENGS) + ["%s_d%d" % (q, r) for q in DMAQ for r in range(NLANE)] + ["cc"]


class Prog:
    def __init__(self):
        self.q = {e: [] for e in ENGS}
        self.val = {s: 0 for s in SEMNAMES}
        self.waited = {e: {o: 0 for o in SEMNAMES} for e in ENGS}
        self.dma_cnt = {q: 0 for q in DMAQ}
        self.lane_last = {q: [None] * NLANE for q in DMAQ}

    def op(self, eng, emit_fn, deps=(), inc=1, sem=None):
        sem = sem or eng
        waits = {}
        for t in deps:
            if t is None:
                continue
            oe, ov = t
            if ov > self.waited[eng][oe]:
                waits[oe] = max(waits.get(oe, 0), ov)
        for oe, ov in waits.items():
            self.waited[eng][oe] = ov
        self.q[eng].append((emit_fn, waits, inc, sem))
        self.val[sem] += inc
        return (sem, self.val[sem])

    def dma_op(self, qeng, emit_fn, deps=()):
        r = self.dma_cnt[qeng] % NLANE
        self.dma_cnt[qeng] += 1
        lane = "%s_d%d" % (qeng, r)
        deps = list(deps)
        if self.lane_last[qeng][r] is not None:
            deps.append(self.lane_last[qeng][r])
        tok = self.op(qeng, emit_fn, deps, inc=16, sem=lane)
        self.lane_last[qeng][r] = tok
        return tok


class Buf:
    def __init__(self, ap=None):
        self.ap = ap
        self.w = None
        self.r = []
        self.epoch = {"lo": None, "hi": None}

    def close_epochs(self):
        if isinstance(self.epoch, dict):
            for h in ("lo", "hi"):
                if self.epoch[h] is not None:
                    self.epoch[h]["stop"] = True
                    self.epoch[h] = None


PROG = None


def emit(eng, fn, ins=(), outs=(), inc=1, dma=False, sem=None):
    deps = []
    for b in ins:
        deps.append(b.w)
        b.close_epochs()
    for b in outs:
        deps.extend(b.r)
        deps.append(b.w)
    if dma:
        tok = PROG.dma_op(eng, fn, deps)
    else:
        tok = PROG.op(eng, fn, deps, inc, sem=sem)
    for b in ins:
        b.r.append(tok)
    for b in outs:
        b.w = tok
        b.r = []
    return tok


def build(n_att_tiles=NT_ATT, n_epi_tiles=NT_EPI, skip_relay=False, n_quant=384, stage=99):
    global PROG
    PROG = Prog()
    nc = bass.Bass("TRN2", target_bir_lowering=False, debug=False)

    x_in = nc.dram_tensor("x", [TPC, D], F32, kind="ExternalInput").ap()
    w_ins = {}
    for nm, shp in (("wqT", [D, D]), ("wkT", [D, D]), ("wvT", [D, D]),
                    ("wgT", [D, D]), ("woT", [D, D]),
                    ("wguT", [D, 2 * HDIM]), ("wpT", [HDIM, D])):
        w_ins[nm] = nc.dram_tensor(nm, shp, F32, kind="ExternalInput").ap()
    gam_in = nc.dram_tensor("gammab", [128, D], F32, kind="ExternalInput").ap()
    bet_in = nc.dram_tensor("betab", [128, D], F32, kind="ExternalInput").ap()
    cmask_in = nc.dram_tensor("cmask", [128, 4], F32, kind="ExternalInput").ap()
    ident_in = nc.dram_tensor("identf", [128, 128], F32, kind="ExternalInput").ap()
    ucum_in = nc.dram_tensor("ucum", [128, 128], BF16, kind="ExternalInput").ap()
    mask_in = nc.dram_tensor("maskbd", [128, 128], F32, kind="ExternalInput").ap()
    ag2_in = nc.dram_tensor("ag2", [128, 2], BF16, kind="ExternalInput").ap()
    agb_in = nc.dram_tensor("agb", [2, 128], F32, kind="ExternalInput").ap()
    ones_in = nc.dram_tensor("onesb", [128, 1], BF16, kind="ExternalInput").ap()
    out_hbm = nc.dram_tensor("out", [TPC, D], F32, kind="ExternalOutput").ap()

    wsp_hi = nc.dram_tensor("wsp_hi", [3, D, D], BF16).ap()
    wsp_lo = nc.dram_tensor("wsp_lo", [3, D, D], BF16).ap()
    wqk_hi = nc.dram_tensor("wqk_hi", [2, H, 16, 128, 128], BF16).ap()
    wqk_lo = nc.dram_tensor("wqk_lo", [2, H, 16, 128, 128], BF16).ap()
    y_sp = nc.dram_tensor("y_sp", [TPC, D], F32).ap()
    qth_sp = nc.dram_tensor("qth_sp", [TPC // 128, 128, D], BF16).ap()
    qtl_sp = nc.dram_tensor("qtl_sp", [TPC // 128, 128, D], BF16).ap()
    wgu_eff = nc.dram_tensor("wgu_eff", [128, 16, 128, 128], BF16).ap()
    wp_eff = nc.dram_tensor("wp_eff", [64, 4, 128, 512], BF16).ap()
    relay_in = nc.dram_tensor("relay_in", [128, D + 16], F32).ap()
    relay_out = nc.dram_tensor("relay_out", [4 * 128, D + 16], F32).ap()

    stack = contextlib.ExitStack()

    def sb(name, p, cols, dt):
        t = stack.enter_context(nc.sbuf_tensor(name, [p, cols], dt))
        return t, Buf(t)

    identf, bident = sb("identf_s", 128, 128, F32)
    ucum, bucum = sb("ucum_s", 128, 128, BF16)
    maskbd, bmask = sb("maskbd_s", 128, 128, F32)
    ag2, bag2 = sb("ag2_s", 128, 2, BF16)
    agb, bagb = sb("agb_s", 2, 128, F32)
    onesb, bones = sb("onesb_s", 128, 1, BF16)
    cmaskb, bcm = sb("cmaskb", 128, 4, F32)

    x2, bx2 = sb("x2", 128, D, F32)
    xnb, bxn = sb("xnb", 128, D, F32)
    xnT_hi, bxnT_hi = sb("xnT_hi", 128, 16 * TT, BF16)
    xnT_lo, bxnT_lo = sb("xnT_lo", 128, 16 * TT, BF16)
    qf, bqf = sb("qf", 128, H * TT, F32)
    kf, bkf = sb("kf", 128, H * TT, F32)
    e2b, be2 = sb("e2b", 128, D, F32)
    gsp, bgsp = sb("gsp", 128, D, F32)
    g_hi, bg_hi = sb("g_hi", 128, D, BF16)
    g_lo, bg_lo = sb("g_lo", 128, D, BF16)
    v_hi, bv_hi = sb("v_hi", 128, D, BF16)
    v_lo, bv_lo = sb("v_lo", 128, D, BF16)
    Sst, bS = sb("Sst", 128, D, F32)
    ktf, bktf = sb("ktf", 128, D, F32)
    khf, bkhf = sb("khf", 128, D, F32)
    kt_hi, bkt_hi = sb("kt_hi", 128, D, BF16)
    kt_lo, bkt_lo = sb("kt_lo", 128, D, BF16)
    kht_hi, bkht_hi = sb("kht_hi", 128, D, BF16)
    kht_lo, bkht_lo = sb("kht_lo", 128, D, BF16)
    qt_hi, bqt_hi = sb("qt_hi", 128, D, BF16)
    qt_lo, bqt_lo = sb("qt_lo", 128, D, BF16)
    pm_hi, bpm_hi = sb("pm_hi", 128, D, BF16)
    pm_lo, bpm_lo = sb("pm_lo", 128, D, BF16)
    st_hi, bst_hi = sb("st_hi", 128, D, BF16)
    st_lo, bst_lo = sb("st_lo", 128, D, BF16)
    wrA_hi, bwrA_hi = sb("wrA_hi", 128, D, BF16)
    wrA_lo, bwrA_lo = sb("wrA_lo", 128, D, BF16)
    wrB_hi, bwrB_hi = sb("wrB_hi", 128, D, BF16)
    wrB_lo, bwrB_lo = sb("wrB_lo", 128, D, BF16)
    nb31, bnb31 = sb("nb31", 128, 32, F32)
    nb63, bnb63 = sb("nb63", 128, 32, F32)
    betlog, bbetlog = sb("betlog", 128, 32 * 16, F32)
    dlog, bdlog = sb("dlog", 128, 32 * 16, F32)
    smalls, bsm = sb("smalls", 128, 64, F32)
    qw, bqw = sb("qw", 128, 1024, F32)
    qA, bqA = sb("qA", 128, 1024, BF16)
    qB, bqB = sb("qB", 128, 1024, BF16)
    qC, bqC = sb("qC", 128, 1024, BF16)
    qD, bqD = sb("qD", 128, 1024, BF16)
    qsc, bqsc = sb("qsc", 2, 512, F32)
    qsci, bqsci = sb("qsci", 2, 512, F32)
    qscb, bqscb = sb("qscb", 2, 512, BF16)
    rly_d, brly_d = sb("rly_d", 128, 16, F32)
    fexp, bfexp = sb("fexp", 128, 16, F32)
    fsum, bfsum = sb("fsum", 128, 16, F32)
    gcur, bgcur = sb("gcur", 128, 16, F32)
    egam, begam = sb("egam", 128, 32, F32)
    rinvt, brinv = sb("rinvt", 128, 8, F32)
    sqs, bsqs = sb("sqs", 1, ET, F32)

    cvals, bcv = sb("cvals", 128, 8, F32)

    pA = stack.enter_context(nc.psum_tensor("pA", [128, 2048], F32))
    pB = stack.enter_context(nc.psum_tensor("pB", [128, 2048], F32))
    bank = [Buf(pA[:, i * 512:(i + 1) * 512]) for i in range(4)] + \
           [Buf(pB[:, i * 512:(i + 1) * 512]) for i in range(4)]

    bconst = Buf(None)
    bwsp = Buf(None); bwqk = Buf(None); bysp = Buf(None)
    bqth = Buf(None); bqtl = Buf(None)
    bwgue = Buf(None); bwpe = Buf(None)
    brin = Buf(None); brout = Buf(None); bout = Buf(None)

    yt_seg = [(qf[:, 0:D], bqf), (qf[:, D:2 * D], bqf),
              (kf[:, 0:D], bkf), (kf[:, D:2 * D], bkf)]
    x1_seg = [(e2b[:, :], be2), (gsp[:, :], bgsp), (Sst[:, :], bS), (xnb[:, :], bxn)]
    ynTh_seg = [(xnT_hi[:, i * 512:(i + 1) * 512], bxnT_hi) for i in range(8)] + \
               [(xnT_lo[:, i * 512:(i + 1) * 512], bxnT_lo) for i in range(8)]
    ynTl_seg = [(v_hi[:, i * 512:(i + 1) * 512], bv_hi) for i in range(4)] + \
               [(v_lo[:, i * 512:(i + 1) * 512], bv_lo) for i in range(4)] + \
               [(g_hi[:, i * 512:(i + 1) * 512], bg_hi) for i in range(4)] + \
               [(g_lo[:, i * 512:(i + 1) * 512], bg_lo) for i in range(4)]
    x2Th_seg = [(kt_hi[:, i * 512:(i + 1) * 512], bkt_hi) for i in range(4)] + \
               [(kt_lo[:, i * 512:(i + 1) * 512], bkt_lo) for i in range(4)] + \
               [(qt_hi[:, i * 512:(i + 1) * 512], bqt_hi) for i in range(4)] + \
               [(qt_lo[:, i * 512:(i + 1) * 512], bqt_lo) for i in range(4)]
    x2Tl_seg = [(kht_hi[:, i * 512:(i + 1) * 512], bkht_hi) for i in range(4)] + \
               [(kht_lo[:, i * 512:(i + 1) * 512], bkht_lo) for i in range(4)] + \
               [(pm_hi[:, i * 512:(i + 1) * 512], bpm_hi) for i in range(4)] + \
               [(pm_lo[:, i * 512:(i + 1) * 512], bpm_lo) for i in range(4)]

    def seg(lst, col, w):
        s, off = divmod(col, 512)
        assert off + w <= 512 or w == 512 and off == 0
        ap_, b_ = lst[s]
        return ap_[:, off:off + w], b_

    gamb, bgam = ktf, bktf
    betb, bbet = khf, bkhf
    qtg_hi, bqtg_hi = st_hi, bst_hi
    qtg_lo, bqtg_lo = st_lo, bst_lo
    sin_b, bsin = x2, bx2
    rly_t, brly_t = khf, bkhf
    ysc, bysc = x2, bx2
    outb, boutb = x2, bx2
    hfb, bhf = qw[:, 0:512], bqw
    h2b, bh2 = qA[:, 0:512], bqA
    ybuf, bybuf = gsp, bgsp
    pmf, bpmf = ktf, bktf
    qtf, bqtf = khf, bkhf
    stfv, bstf = xnb, bxn
    fwg = [wrA_hi, wrA_lo]; bfwg = [bwrA_hi, bwrA_lo]
    fwu = [wrB_hi, wrB_lo]; bfwu = [bwrB_hi, bwrB_lo]
    fwp = [qw[:, :].bitcast(BF16), khf[:, :].bitcast(BF16)[:, 0:2048]]
    bfwp = [bqw, bkhf]
    c0_hi_seg = [(qA[:, :], bqA), (qB[:, :], bqB)]
    c0_lo_seg = [(qC[:, :], bqC), (qD[:, :], bqD)]

    def dma(queue, dst_ap, src_ap, ins, outs):
        def fn(eng):
            return eng.dma_start(out=dst_ap, in_=src_ap)
        return emit(queue, fn, ins, outs, dma=True)

    def mmb(bankbuf, out_ap, lhsT, rhs, in_bufs, tp=None, half="both"):
        halves = ("lo", "hi") if half == "both" else (half,)
        rec = {"start": all(bankbuf.epoch[h] is None for h in halves), "stop": False}
        for h in halves:
            bankbuf.epoch[h] = rec

        def fn(eng):
            return eng.matmul(out_ap, lhsT, rhs, start=rec["start"], stop=rec["stop"],
                              tile_position=tp, skip_group_check=True)
        return emit("tensor", fn, in_bufs, [bankbuf], inc=1)

    def transp(in_ap, out_ap, bankbuf, in_bufs, tp=None):
        bankbuf.close_epochs()
        n = in_ap.shape[0]

        def fn(eng):
            return eng.transpose(out_ap, in_ap, identf[0:n, 0:n], tile_position=tp)
        return emit("tensor", fn, list(in_bufs) + [bident], [bankbuf], inc=1)

    def v_tt(out_ap, a_ap, b_ap, op, ins, outs, eng="vector"):
        def fn(e):
            return e.tensor_tensor(out_ap, a_ap, b_ap, op)
        return emit(eng, fn, ins, outs)

    def v_ts(out_ap, a_ap, s1, ins, outs, op0=AL.mult, s2=None, op1=AL.bypass, eng="vector"):
        def fn(e):
            return e.tensor_scalar(out_ap, a_ap, s1, s2, op0, op1)
        return emit(eng, fn, ins, outs)

    def v_stt(out_ap, a_ap, s, b_ap, op0, op1, ins, outs, eng="vector"):
        def fn(e):
            return e.scalar_tensor_tensor(out_ap, a_ap, s, b_ap, op0, op1)
        return emit(eng, fn, ins, outs)

    def v_copy(out_ap, a_ap, ins, outs, eng="vector"):
        def fn(e):
            return e.tensor_copy(out_ap, a_ap)
        return emit(eng, fn, ins, outs)

    def v_red(out_ap, a_ap, op, ins, outs):
        def fn(e):
            return e.tensor_reduce(out_ap, a_ap, AX.X, op)
        return emit("vector", fn, ins, outs)

    def v_recip(out_ap, a_ap, ins, outs):
        def fn(e):
            return e.reciprocal(out_ap, a_ap)
        return emit("vector", fn, ins, outs)

    def s_act(out_ap, a_ap, func, ins, outs, bias=0.0, scale=1.0, accum=None):
        def fn(e):
            return e.activation(out_ap, a_ap, func, bias=bias, scale=scale, accum_out=accum)
        return emit("scalar", fn, list(ins) + [bcv], outs)

    def s_mul(out_ap, a_ap, m, ins, outs):
        def fn(e):
            return e.mul(out_ap, a_ap, m)
        return emit("scalar", fn, list(ins) + [bcv], outs)

    def v_memset(ap_, val, outs, eng="vector"):
        def fn(e):
            return e.memset(ap_, val)
        return emit(eng, fn, (), outs)

    _cv = [float(EPS_LN), 1.0, float(EPS_RMS), -1.0, 1.0 / HDIM]
    for i, v in enumerate(_cv):
        def mkset(ap_, val):
            def fn(e):
                return e.memset(ap_, val)
            return fn
        emit("vector", mkset(cvals[:, i:i + 1], v), (), [bcv])
        nc.const_aps.register(F32, v, cvals[:, i:i + 1])

    for dst, bdst, src in ((identf, bident, ident_in), (ucum, bucum, ucum_in),
                           (maskbd, bmask, mask_in), (ag2, bag2, ag2_in),
                           (agb, bagb, agb_in), (onesb, bones, ones_in),
                           (cmaskb, bcm, cmask_in)):
        dma("sync", dst[:, :], src[:, :], [bconst], [bdst])
    v_memset(Sst[:, :], 0.0, [bS])
    v_memset(dlog[:, :], 0.0, [bdlog])
    v_memset(betlog[:, :], 0.0, [bbetlog])

    for (nm, cs, mode, mi) in (("wqT", CSCALE, "qk", 0), ("wkT", 1.0, "qk", 1),
                               ("wgT", 1.0, "row", 0), ("wvT", 1.0, "row", 1),
                               ("woT", 1.0, "row", 2)):
        wap = w_ins[nm]
        for dc in range(16):
            dma("sync", x2[:, :], wap[dc * 128:(dc + 1) * 128, :], [bconst], [bx2])
            v_ts(wrA_hi[:, :], x2[:, :], float(cs), [bx2], [bwrA_hi])
            v_stt(wrA_lo[:, :], x2[:, :], float(cs), wrA_hi[:, :], AL.mult, AL.subtract,
                  [bx2, bwrA_hi], [bwrA_lo])
            if mode == "row":
                dma("sync", wsp_hi[mi, dc * 128:(dc + 1) * 128, :], wrA_hi[:, :],
                    [bwrA_hi], [bwsp])
                dma("sync", wsp_lo[mi, dc * 128:(dc + 1) * 128, :], wrA_lo[:, :],
                    [bwrA_lo], [bwsp])
            else:
                dma("sync", wqk_hi[mi, :, dc, :, :].transpose([1, 0, 2]),
                    wrA_hi[:, :].rearrange("p (h c) -> p h c", h=16), [bwrA_hi], [bwqk])
                dma("sync", wqk_lo[mi, :, dc, :, :].transpose([1, 0, 2]),
                    wrA_lo[:, :].rearrange("p (h c) -> p h c", h=16), [bwrA_lo], [bwqk])

    qstate = {"i": 0}

    def emit_quant_tile(idx):
        if idx < 256:
            rc, ct = idx // 16, idx % 16
            src = w_ins["wguT"][rc * 128:(rc + 1) * 128, ct * 1024:(ct + 1) * 1024]
            is_gu = True
        else:
            t_ = idx - 256
            rc, ct = t_ // 2, t_ % 2
            src = w_ins["wpT"][rc * 128:(rc + 1) * 128, ct * 1024:(ct + 1) * 1024]
            is_gu = False
        dma("gpsimd", qw[:, :], src, [bconst], [bqw])
        v_copy(qA[:, :], qw[:, :], [bqw], [bqA])
        s_act(qB[:, :], qA[:, :], AF.Abs, [bqA], [bqB])
        pb = bank[6]
        binv = [bank[7], bank[4]]
        bscl = [bank[5], bank[3]]
        for s4 in range(2):
            sl = slice(s4 * 512, (s4 + 1) * 512)
            mmb(pb, pb.ap[0:2, :], ag2[:, :], qB[:, sl], [bag2, bqB], half="lo")
            v_ts(qscb[0:2, :], pb.ap[0:2, :], 1.0 / GS, [pb], [bqscb], op0=AL.mult,
                 s2=float(np.float32(ml_dtypes.bfloat16(1e-8))), op1=AL.max)
            v_copy(qsc[0:2, :], qscb[0:2, :], [bqscb], [bqsc])
            v_recip(qsci[0:2, :], qsc[0:2, :], [bqsc], [bqsci])
            mmb(binv[s4], binv[s4].ap[:, :], agb[0:2, :], qsci[0:2, :], [bagb, bqsci])
            mmb(bscl[s4], bscl[s4].ap[:, :], agb[0:2, :], qsc[0:2, :], [bagb, bqsc])
        for s4 in range(2):
            v_tt(qB[:, s4 * 512:(s4 + 1) * 512], qA[:, s4 * 512:(s4 + 1) * 512],
                 binv[s4].ap[:, :], AL.mult, [bqA, binv[s4]], [bqB])
        v_ts(qC[:, :], qB[:, :], -0.5, [bqB], [bqC], op0=AL.is_lt)
        v_stt(qD[:, :], qB[:, :], 0.5, qC[:, :], AL.is_gt, AL.subtract,
              [bqB, bqC], [bqD])
        for s4 in range(2):
            v_tt(qB[:, s4 * 512:(s4 + 1) * 512], qD[:, s4 * 512:(s4 + 1) * 512],
                 bscl[s4].ap[:, :], AL.mult, [bqD, bscl[s4]], [bqB])
        v_tt(qC[:, :], qB[:, :], qA[:, :], AL.subtract, [bqB, bqA], [bqC], eng="gpsimd")
        v_tt(qB[:, :], qA[:, :], qC[:, :], AL.add, [bqA, bqC], [bqB], eng="gpsimd")
        if is_gu:
            hb0 = ct * 8
            dst = wgu_eff[hb0:hb0 + 8, rc, :, :].transpose([1, 0, 2])
            dma("gpsimd", dst.bitcast(U16),
                qB[:, :].rearrange("p (hb c) -> p hb c", hb=8).bitcast(U16),
                [bqB], [bwgue])
        else:
            dc0 = ct * 2
            dst = wp_eff[rc, dc0:dc0 + 2, :, :].transpose([1, 0, 2])
            dma("gpsimd", dst.bitcast(U16),
                qB[:, :].rearrange("p (dc c) -> p dc c", dc=2).bitcast(U16),
                [bqB], [bwpe])

    quant_per_tile = (n_quant + n_att_tiles - 1) // max(1, n_att_tiles)

    def ln_stats_apply(src_buf, src_ap, dst_buf, dst_ap, eps, dump=None):
        v_red(smalls[:, 0:1], src_ap, AL.add, [src_buf], [bsm])
        for bb in (bank[0], bank[1], bank[2], bank[3]):
            bb.close_epochs()
        s_act(pA[:, :], src_ap, AF.Square,
              [src_buf], [bank[0], bank[1], bank[2], bank[3], bsm],
              accum=smalls[:, 1:2])
        v_ts(smalls[:, 2:3], smalls[:, 0:1], 1.0 / D, [bsm], [bsm])
        v_ts(smalls[:, 3:4], smalls[:, 1:2], 1.0 / D, [bsm], [bsm])
        v_tt(smalls[:, 4:5], smalls[:, 2:3], smalls[:, 2:3], AL.mult, [bsm], [bsm])
        v_tt(smalls[:, 5:6], smalls[:, 3:4], smalls[:, 4:5], AL.subtract, [bsm], [bsm])
        s_act(smalls[:, 6:7], smalls[:, 5:6], AF.Sqrt, [bsm], [bsm], bias=float(eps))
        v_recip(smalls[:, 7:8], smalls[:, 6:7], [bsm], [bsm])
        v_stt(smalls[:, 8:9], smalls[:, 2:3], -1.0, smalls[:, 7:8], AL.mult, AL.mult,
              [bsm], [bsm])
        s_act(dst_ap, src_ap, AF.Identity, [src_buf, bsm], [dst_buf],
              bias=smalls[:, 8:9], scale=smalls[:, 7:8])

    qfv = qf.rearrange("p (h t) -> p h t", h=H)
    kfv = kf.rearrange("p (h t) -> p h t", h=H)
    e2v = e2b.rearrange("p (h t) -> p h t", h=H)
    sv = Sst.rearrange("p (h t) -> p h t", h=H)
    WQ = "sync"

    for t in range(n_att_tiles):
        for tb in range(2):
            pair = t * 2 + tb
            dma("scalar", x2[:, :], x_in[pair * 128:(pair + 1) * 128, :], [bconst], [bx2])
            ln_stats_apply(bx2, x2[:, :], bxn, xnb[:, :], EPS_LN)
            for dc in range(16):
                pb = bank[dc % 2]
                transp(xnb[:, dc * 128:(dc + 1) * 128], pb.ap[:, 0:128], pb, [bxn])
                col = dc * TT + tb * 128
                v_copy(xnT_hi[:, col:col + 128], pb.ap[:, 0:128], [pb], [bxnT_hi])
                v_stt(xnT_lo[:, col:col + 128], pb.ap[:, 0:128], 1.0,
                      xnT_hi[:, col:col + 128], AL.mult, AL.subtract,
                      [pb, bxnT_hi], [bxnT_lo])

        for half in range(2 if stage >= 2 else 0):
            for hh in range(8):
                h = half * 8 + hh
                dma(WQ, wrA_hi[:, :].rearrange("p (dc c) -> p dc c", dc=16),
                    wqk_hi[0, h, :, :, :].transpose([1, 0, 2]), [bwqk], [bwrA_hi])
                dma(WQ, wrA_lo[:, :].rearrange("p (dc c) -> p dc c", dc=16),
                    wqk_lo[0, h, :, :, :].transpose([1, 0, 2]), [bwqk], [bwrA_lo])
                dma(WQ, wrB_hi[:, :].rearrange("p (dc c) -> p dc c", dc=16),
                    wqk_hi[1, h, :, :, :].transpose([1, 0, 2]), [bwqk], [bwrB_hi])
                dma(WQ, wrB_lo[:, :].rearrange("p (dc c) -> p dc c", dc=16),
                    wqk_lo[1, h, :, :, :].transpose([1, 0, 2]), [bwqk], [bwrB_lo])
                qb = bank[hh // 2]
                kb = bank[4 + hh // 2]
                oc = (hh % 2) * 256
                for dc in range(16):
                    xh = xnT_hi[:, dc * TT:(dc + 1) * TT]
                    xl = xnT_lo[:, dc * TT:(dc + 1) * TT]
                    w0 = wrA_hi[:, dc * 128:(dc + 1) * 128]
                    w1 = wrA_lo[:, dc * 128:(dc + 1) * 128]
                    mmb(qb, qb.ap[:, oc:oc + 256], w0, xh, [bwrA_hi, bxnT_hi])
                    mmb(qb, qb.ap[:, oc:oc + 256], w0, xl, [bwrA_hi, bxnT_lo])
                    mmb(qb, qb.ap[:, oc:oc + 256], w1, xh, [bwrA_lo, bxnT_hi])
                for dc in range(16):
                    xh = xnT_hi[:, dc * TT:(dc + 1) * TT]
                    xl = xnT_lo[:, dc * TT:(dc + 1) * TT]
                    w0 = wrB_hi[:, dc * 128:(dc + 1) * 128]
                    w1 = wrB_lo[:, dc * 128:(dc + 1) * 128]
                    mmb(kb, kb.ap[:, oc:oc + 256], w0, xh, [bwrB_hi, bxnT_hi])
                    mmb(kb, kb.ap[:, oc:oc + 256], w0, xl, [bwrB_hi, bxnT_lo])
                    mmb(kb, kb.ap[:, oc:oc + 256], w1, xh, [bwrB_lo, bxnT_hi])
                if hh % 2 == 1:
                    for hh2 in (hh - 1, hh):
                        h2 = half * 8 + hh2
                        oc2 = (hh2 % 2) * 256
                        v_copy(qfv[:, h2, :], bank[hh // 2].ap[:, oc2:oc2 + 256],
                               [bank[hh // 2]], [bqf])
                        v_copy(kfv[:, h2, :], bank[4 + hh // 2].ap[:, oc2:oc2 + 256],
                               [bank[4 + hh // 2]], [bkf])

        for tb in range(2 if stage >= 3 else 0):
            pair = t * 2 + tb

            for dc in range(16):
                wh, bwh = (wrA_hi, bwrA_hi) if dc % 2 == 0 else (wrB_hi, bwrB_hi)
                wl, bwl = (wrA_lo, bwrA_lo) if dc % 2 == 0 else (wrB_lo, bwrB_lo)
                dma(WQ, wh[:, :], wsp_hi[0, dc * 128:(dc + 1) * 128, :], [bwsp], [bwh])
                dma(WQ, wl[:, :], wsp_lo[0, dc * 128:(dc + 1) * 128, :], [bwsp], [bwl])
                col = dc * TT + tb * 128
                xh = xnT_hi[:, col:col + 128]
                xl = xnT_lo[:, col:col + 128]
                if stage < 3.005:
                    continue
                for nt in range(4):
                    pb = bank[4 + nt]
                    mmb(pb, pb.ap[:, :], xh, wh[:, nt * 512:(nt + 1) * 512], [bxnT_hi, bwh])
                    mmb(pb, pb.ap[:, :], xh, wl[:, nt * 512:(nt + 1) * 512], [bxnT_hi, bwl])
                    mmb(pb, pb.ap[:, :], xl, wh[:, nt * 512:(nt + 1) * 512], [bxnT_lo, bwh])
            for nt in range(4 if stage >= 3.01 else 0):
                pb = bank[4 + nt]
                sl = slice(nt * 512, (nt + 1) * 512)
                v_ts(gsp[:, sl], pb.ap[:, :], 0.0, [pb], [bgsp], op0=AL.min)
                if stage < 3.02:
                    continue
                v_stt(xnb[:, sl], gsp[:, sl], 2.0, pb.ap[:, :], AL.mult, AL.subtract,
                      [bgsp, pb], [bxn])
            if stage >= 3.04:
                s_act(xnb[:, :], xnb[:, :], AF.Exp, [bxn], [bxn])
            if stage >= 3.05:
                s_act(xnb[:, :], xnb[:, :], AF.Ln, [bxn], [bxn], bias=1.0)
            if stage >= 3.06:
                v_tt(gsp[:, :], gsp[:, :], xnb[:, :], AL.subtract, [bgsp, bxn], [bgsp])
            if stage < 3.2:
                continue
            v_copy(g_hi[:, :], gsp[:, :], [bgsp], [bg_hi])
            v_stt(g_lo[:, :], gsp[:, :], 1.0, g_hi[:, :], AL.mult, AL.subtract,
                  [bgsp, bg_hi], [bg_lo])

            for grp in range(2):
                pbs = [bank[4 + 2 * grp], bank[5 + 2 * grp]]
                for hh in range(8):
                    h = grp * 8 + hh
                    pb = pbs[hh // 4]
                    oc = (hh % 4) * 128
                    mmb(pb, pb.ap[:, oc:oc + 128], g_hi[:, h * 128:(h + 1) * 128],
                        ucum[:, :], [bg_hi, bucum])
                    mmb(pb, pb.ap[:, oc:oc + 128], g_lo[:, h * 128:(h + 1) * 128],
                        ucum[:, :], [bg_lo, bucum])
                for hh in range(8 if stage >= 3.5 else 0):
                    h = grp * 8 + hh
                    pb = pbs[hh // 4]
                    oc = (hh % 4) * 128
                    for cc in range(2):
                        gj = pair * 2 + cc
                        c31 = oc + cc * 64 + 31
                        c63 = oc + cc * 64 + 63
                        v_ts(nb31[:, cc * 16 + h:cc * 16 + h + 1], pb.ap[:, c31:c31 + 1],
                             -1.0, [pb], [bnb31])
                        v_ts(nb63[:, cc * 16 + h:cc * 16 + h + 1], pb.ap[:, c63:c63 + 1],
                             -1.0, [pb], [bnb63])
                        v_ts(betlog[:, gj * 16 + h:gj * 16 + h + 1], pb.ap[:, c31:c31 + 1],
                             1.0, [pb], [bbetlog])
                        v_ts(dlog[:, gj * 16 + h:gj * 16 + h + 1], pb.ap[:, c63:c63 + 1],
                             1.0, [pb], [bdlog])
                for hh in range(8):
                    h = grp * 8 + hh
                    pb = pbs[hh // 4]
                    oc = (hh % 4) * 128
                    for cc in range(2):
                        s_act(e2v[:, h, cc * 64:(cc + 1) * 64],
                              pb.ap[:, oc + cc * 64:oc + cc * 64 + 64], AF.Exp,
                              [pb, bnb31], [be2],
                              bias=nb31[:, cc * 16 + h:cc * 16 + h + 1])

            for dc in range(16 if stage >= 3.8 else 0):
                wh, bwh = (wrA_hi, bwrA_hi) if dc % 2 == 0 else (wrB_hi, bwrB_hi)
                wl, bwl = (wrA_lo, bwrA_lo) if dc % 2 == 0 else (wrB_lo, bwrB_lo)
                dma(WQ, wh[:, :], wsp_hi[1, dc * 128:(dc + 1) * 128, :], [bwsp], [bwh])
                dma(WQ, wl[:, :], wsp_lo[1, dc * 128:(dc + 1) * 128, :], [bwsp], [bwl])
                col = dc * TT + tb * 128
                xh = xnT_hi[:, col:col + 128]
                xl = xnT_lo[:, col:col + 128]
                for nt in range(4):
                    pb = bank[4 + nt]
                    mmb(pb, pb.ap[:, :], xh, wh[:, nt * 512:(nt + 1) * 512], [bxnT_hi, bwh])
                    mmb(pb, pb.ap[:, :], xh, wl[:, nt * 512:(nt + 1) * 512], [bxnT_hi, bwl])
                    mmb(pb, pb.ap[:, :], xl, wh[:, nt * 512:(nt + 1) * 512], [bxnT_lo, bwh])
            for nt in range(4):
                pb = bank[4 + nt]
                sl = slice(nt * 512, (nt + 1) * 512)
                v_copy(v_hi[:, sl], pb.ap[:, :], [pb], [bv_hi])
                v_stt(v_lo[:, sl], pb.ap[:, :], 1.0, v_hi[:, sl], AL.mult, AL.subtract,
                      [pb, bv_hi], [bv_lo])

            if stage < 4:
                continue
            ktv3 = ktf.rearrange("p (h t) -> p h t", h=H)
            e1ap = x2.rearrange("p (h t) -> p h t", h=H)
            v_recip(e1ap, e2v, [be2], [bx2])
            v_tt(ktv3, kfv[:, :, tb * 128:(tb + 1) * 128], e1ap, AL.mult,
                 [bkf, bx2], [bktf])
            v_copy(kt_hi[:, :], ktf[:, :], [bktf], [bkt_hi])
            v_stt(kt_lo[:, :], ktf[:, :], 1.0, kt_hi[:, :], AL.mult, AL.subtract,
                  [bktf, bkt_hi], [bkt_lo])
            qtv3 = qtf.rearrange("p (h t) -> p h t", h=H)
            v_tt(qtv3, qfv[:, :, tb * 128:(tb + 1) * 128], e2v, AL.mult,
                 [bqf, be2], [bqtf])
            v_copy(qt_hi[:, :], qtf[:, :], [bqtf], [bqt_hi])
            v_stt(qt_lo[:, :], qtf[:, :], 1.0, qt_hi[:, :], AL.mult, AL.subtract,
                  [bqtf, bqt_hi], [bqt_lo])
            dma("gpsimd", qth_sp[pair, :, :], qt_hi[:, :], [bqt_hi], [bqth])
            dma("gpsimd", qtl_sp[pair, :, :], qt_lo[:, :], [bqt_lo], [bqtl])
            v_tt(smalls[:, 16:48], nb31[:, 0:32], nb63[:, 0:32], AL.subtract,
                 [bnb31, bnb63], [bsm])
            s_act(smalls[:, 16:48], smalls[:, 16:48], AF.Exp, [bsm], [bsm])
            e4v = smalls[:, 16:48].rearrange("p (c h) -> p c h", c=2).transpose([0, 2, 1])
            e4bc = e4v.unsqueeze(3).broadcast_to([128, H, 2, 64])
            v_tt(khf.rearrange("p (h c t) -> p h c t", h=H, c=2),
                 ktf.rearrange("p (h c t) -> p h c t", h=H, c=2), e4bc, AL.mult,
                 [bktf, bsm], [bkhf])
            khv = khf.rearrange("p (h t) -> p h t", h=H)
            for h in range(H):
                pb = bank[h % 2]
                transp(khv[:, h, 0:64], pb.ap[0:64, 0:128], pb, [bkhf])
                v_copy(kht_hi[0:64, h * 128:(h + 1) * 128], pb.ap[0:64, 0:128],
                       [pb], [bkht_hi])
                v_stt(kht_lo[0:64, h * 128:(h + 1) * 128], pb.ap[0:64, 0:128], 1.0,
                      kht_hi[0:64, h * 128:(h + 1) * 128], AL.mult, AL.subtract,
                      [pb, bkht_hi], [bkht_lo])

            qtv = qt_hi.rearrange("p (h t) -> p h t", h=H)
            qlv = qt_lo.rearrange("p (h t) -> p h t", h=H)
            ktv = kt_hi.rearrange("p (h t) -> p h t", h=H)
            klv = kt_lo.rearrange("p (h t) -> p h t", h=H)
            vhv = v_hi.rearrange("p (h t) -> p h t", h=H)
            vlv = v_lo.rearrange("p (h t) -> p h t", h=H)

            for grp in range(2):
                pbs = [bank[2 * grp], bank[2 * grp + 1]]
                for hh in range(8):
                    h = grp * 8 + hh
                    pb = pbs[hh // 4]
                    oc = (hh % 4) * 128
                    mmb(pb, pb.ap[:, oc:oc + 128], ktv[:, h, :], qtv[:, h, :],
                        [bkt_hi, bqt_hi])
                    mmb(pb, pb.ap[:, oc:oc + 128], ktv[:, h, :], qlv[:, h, :],
                        [bkt_hi, bqt_lo])
                    mmb(pb, pb.ap[:, oc:oc + 128], klv[:, h, :], qtv[:, h, :],
                        [bkt_lo, bqt_hi])
                for half in range(2):
                    pb = pbs[half]
                    h0 = grp * 8 + half * 4
                    mbc = maskbd[:, :].unsqueeze(1).broadcast_to([128, 4, 128])
                    v_tt(pmf.rearrange("p (h t) -> p h t", h=H)[:, h0:h0 + 4, :],
                         pb.ap[:, :].rearrange("p (h t) -> p h t", h=4), mbc, AL.mult,
                         [pb, bmask], [bpmf])
            v_copy(pm_hi[:, :], pmf[:, :], [bpmf], [bpm_hi])
            v_stt(pm_lo[:, :], pmf[:, :], 1.0, pm_hi[:, :], AL.mult, AL.subtract,
                  [bpmf, bpm_hi], [bpm_lo])

            ybanks = [bank[4], bank[5], bank[6], bank[7]]
            pmhv = pm_hi.rearrange("p (h t) -> p h t", h=H)
            pmlv = pm_lo.rearrange("p (h t) -> p h t", h=H)
            for h in range(H):
                yb = ybanks[h // 4]
                oc = (h % 4) * 128
                mmb(yb, yb.ap[:, oc:oc + 128], pmhv[:, h, :], vhv[:, h, :],
                    [bpm_hi, bv_hi])
                mmb(yb, yb.ap[:, oc:oc + 128], pmhv[:, h, :], vlv[:, h, :],
                    [bpm_hi, bv_lo])
                mmb(yb, yb.ap[:, oc:oc + 128], pmlv[:, h, :], vhv[:, h, :],
                    [bpm_lo, bv_hi])

            v_copy(ybuf[:, :], pB[:, :], [bank[4], bank[5], bank[6], bank[7]], [bybuf])
            khtv_h = kht_hi.rearrange("p (h t) -> p h t", h=H)
            khtv_l = kht_lo.rearrange("p (h t) -> p h t", h=H)
            stv_h = st_hi.rearrange("p (h t) -> p h t", h=H)
            stv_l = st_lo.rearrange("p (h t) -> p h t", h=H)
            stf3 = stfv.rearrange("p (h t) -> p h t", h=H)
            ubanks = [bank[0], bank[1], bank[2], bank[3]]
            y2banks = [bank[4], bank[5], bank[6], bank[7]]
            for cc in range(2):
                e5 = smalls[:, 48:64]
                s_act(e5, nb31[:, cc * 16:(cc + 1) * 16], AF.Exp, [bnb31], [bsm],
                      scale=-1.0)
                e5bc = e5.unsqueeze(2).broadcast_to([128, H, 128])
                v_tt(stf3, sv, e5bc, AL.mult, [bS, bsm], [bstf])
                v_copy(st_hi[:, :], stfv[:, :], [bstf], [bst_hi])
                v_stt(st_lo[:, :], stfv[:, :], 1.0, st_hi[:, :], AL.mult, AL.subtract,
                      [bstf, bst_hi], [bst_lo])
                for h in range(H):
                    yb = y2banks[h // 4]
                    oc = (h % 4) * 128
                    qh_ = qtv[:, h, cc * 64:cc * 64 + 64]
                    ql_ = qlv[:, h, cc * 64:cc * 64 + 64]
                    orows = yb.ap[0:64, oc:oc + 128]
                    mmb(yb, orows, qh_, stv_h[:, h, :], [bqt_hi, bst_hi], half="lo")
                    mmb(yb, orows, qh_, stv_l[:, h, :], [bqt_hi, bst_lo], half="lo")
                    mmb(yb, orows, ql_, stv_h[:, h, :], [bqt_lo, bst_hi], half="lo")
                if cc == 0:
                    v_tt(ybuf[0:64, :], ybuf[0:64, :], pB[0:64, :], AL.add,
                         [bank[4], bank[5], bank[6], bank[7], bybuf], [bybuf])
                else:
                    v_copy(stfv[0:64, :], pB[0:64, :],
                           [bank[4], bank[5], bank[6], bank[7]], [bstf])
                    dma("gpsimd", stfv[64:128, :], stfv[0:64, :], [bstf], [bstf])
                    v_tt(ybuf[64:128, :], ybuf[64:128, :], stfv[64:128, :], AL.add,
                         [bstf, bybuf], [bybuf])
                if cc == 0:
                    for h in range(H):
                        ub = ubanks[h // 4]
                        oc = (h % 4) * 128
                        mmb(ub, ub.ap[:, oc:oc + 128], khtv_h[0:64, h, :],
                            vhv[0:64, h, :], [bkht_hi, bv_hi])
                        mmb(ub, ub.ap[:, oc:oc + 128], khtv_h[0:64, h, :],
                            vlv[0:64, h, :], [bkht_hi, bv_lo])
                        mmb(ub, ub.ap[:, oc:oc + 128], khtv_l[0:64, h, :],
                            vhv[0:64, h, :], [bkht_lo, bv_hi])
                else:
                    dma("gpsimd", qt_hi[0:64, :], v_hi[64:128, :], [bv_hi], [bqt_hi])
                    dma("gpsimd", qt_lo[0:64, :], v_lo[64:128, :], [bv_lo], [bqt_lo])
                    for h in range(H):
                        pb = bank[h % 2]
                        transp(khv[:, h, 64:128], pb.ap[0:64, 0:128], pb, [bkhf])
                        v_copy(kht_hi[0:64, h * 128:(h + 1) * 128], pb.ap[0:64, 0:128],
                               [pb], [bkht_hi])
                        v_stt(kht_lo[0:64, h * 128:(h + 1) * 128], pb.ap[0:64, 0:128],
                              1.0, kht_hi[0:64, h * 128:(h + 1) * 128],
                              AL.mult, AL.subtract, [pb, bkht_hi], [bkht_lo])
                    vh2 = qt_hi.rearrange("p (h t) -> p h t", h=H)
                    vl2 = qt_lo.rearrange("p (h t) -> p h t", h=H)
                    for h in range(H):
                        ub = ubanks[h // 4]
                        oc = (h % 4) * 128
                        mmb(ub, ub.ap[:, oc:oc + 128], khtv_h[0:64, h, :],
                            vh2[0:64, h, :], [bkht_hi, bqt_hi])
                        mmb(ub, ub.ap[:, oc:oc + 128], khtv_h[0:64, h, :],
                            vl2[0:64, h, :], [bkht_hi, bqt_lo])
                        mmb(ub, ub.ap[:, oc:oc + 128], khtv_l[0:64, h, :],
                            vh2[0:64, h, :], [bkht_lo, bqt_hi])
                e3 = smalls[:, 0:16]
                s_act(e3, nb63[:, cc * 16:(cc + 1) * 16], AF.Exp, [bnb63], [bsm],
                      scale=-1.0)
                for h in range(H):
                    ub = ubanks[h // 4]
                    oc = (h % 4) * 128
                    v_stt(sv[:, h, :], sv[:, h, :], e3[:, h:h + 1], ub.ap[:, oc:oc + 128],
                          AL.mult, AL.add, [bsm, ub], [bS])
            dma("scalar", y_sp[pair * 128:(pair + 1) * 128, :], ybuf[:, :], [bybuf], [bysp])

        for _ in range(quant_per_tile):
            if qstate["i"] < n_quant:
                emit_quant_tile(qstate["i"])
                qstate["i"] += 1

    while qstate["i"] < n_quant:
        emit_quant_tile(qstate["i"])
        qstate["i"] += 1

    dv = dlog.rearrange("p (j h) -> p h j", j=32)
    v_red(rly_d[:, :], dv, AL.add, [bdlog], [brly_d])
    if not skip_relay:
        dma("gpsimd", relay_in[:, 0:D], Sst[:, :], [bS], [brin])
        dma("gpsimd", relay_in[:, D:D + 16], rly_d[:, :], [brly_d], [brin])

        def cc_fn(eng):
            return eng.collective_compute(
                "AllGather", AL.bypass,
                replica_groups=[[0, 1, 2, 3], [4, 5, 6, 7]],
                ins=[relay_in[:, :].opt()],
                outs=[relay_out[:, :].opt()])
        emit("gpsimd", cc_fn, [brin], [brout], inc=1, sem="cc")
        v_memset(sin_b[:, :], 0.0, [bsin])
        v_memset(fsum[:, :], 0.0, [bfsum])
        for jj in (2, 1, 0):
            dma("gpsimd", rly_t[:, :], relay_out[jj * 128:(jj + 1) * 128, 0:D],
                [brout], [brly_t])
            dma("gpsimd", rly_d[:, :], relay_out[jj * 128:(jj + 1) * 128, D:D + 16],
                [brout], [brly_d])
            s_act(fexp[:, :], fsum[:, :], AF.Exp, [bfsum], [bfexp])
            fbc = fexp[:, :].unsqueeze(2).broadcast_to([128, H, 128])
            v_tt(stf3, rly_t.rearrange("p (h t) -> p h t", h=H), fbc, AL.mult,
                 [brly_t, bfexp], [bstf])
            v_stt(sin_b[:, :], stfv[:, :], cmaskb[:, jj:jj + 1], sin_b[:, :],
                  AL.mult, AL.add, [bstf, bcm], [bsin])
            v_stt(fsum[:, :], rly_d[:, :], cmaskb[:, jj:jj + 1], fsum[:, :],
                  AL.mult, AL.add, [brly_d, bcm], [bfsum])
    else:
        v_memset(sin_b[:, :], 0.0, [bsin])
    for sgi in range(2):
        chap, chb = c0_hi_seg[sgi]
        clap, clb = c0_lo_seg[sgi]
        sl = slice(sgi * 1024, (sgi + 1) * 1024)
        v_copy(chap, sin_b[:, sl], [bsin], [chb])
        v_stt(clap, sin_b[:, sl], 1.0, chap, AL.mult, AL.subtract, [bsin, chb], [clb])
    v_memset(gcur[:, :], 0.0, [bgcur])

    for t in range(n_epi_tiles):
        tok0 = t * ET
        for tb in range(4):
            ap_, b_ = yt_seg[tb]
            dma("scalar", ap_, y_sp[tok0 + tb * 128:tok0 + tb * 128 + 128, :],
                [bysp], [b_])
        for pp in range(4):
            pair = t * 4 + pp
            dma("gpsimd", qt_hi[:, :], qth_sp[pair, :, :], [bqth], [bqt_hi])
            dma("gpsimd", qt_lo[:, :], qtl_sp[pair, :, :], [bqtl], [bqt_lo])
            for cc in range(2):
                gj = pair * 2 + cc
                v_tt(egam[:, cc * 16:(cc + 1) * 16], betlog[:, gj * 16:(gj + 1) * 16],
                     gcur[:, :], AL.add, [bbetlog, bgcur], [begam])
                s_act(egam[:, cc * 16:(cc + 1) * 16], egam[:, cc * 16:(cc + 1) * 16],
                      AF.Exp, [begam], [begam])
                v_tt(gcur[:, :], gcur[:, :], dlog[:, gj * 16:(gj + 1) * 16], AL.add,
                     [bgcur, bdlog], [bgcur])
            egv = egam.rearrange("p (c h) -> p c h", c=2).transpose([0, 2, 1])
            egbc = egv.unsqueeze(3).broadcast_to([128, H, 2, 64])
            v_tt(qtg_hi.rearrange("p (h c t) -> p h c t", h=H, c=2),
                 qt_hi.rearrange("p (h c t) -> p h c t", h=H, c=2), egbc, AL.mult,
                 [bqt_hi, begam], [bqtg_hi])
            v_tt(qtg_lo.rearrange("p (h c t) -> p h c t", h=H, c=2),
                 qt_lo.rearrange("p (h c t) -> p h c t", h=H, c=2), egbc, AL.mult,
                 [bqt_lo, begam], [bqtg_lo])
            dbanks = [bank[0], bank[1], bank[2], bank[3]]
            qgv = qtg_hi.rearrange("p (h t) -> p h t", h=H)
            qgl = qtg_lo.rearrange("p (h t) -> p h t", h=H)
            def c0h_f(h):
                ap_, b_ = c0_hi_seg[h // 8]
                return ap_[:, (h % 8) * 128:(h % 8 + 1) * 128], b_
            def c0l_f(h):
                ap_, b_ = c0_lo_seg[h // 8]
                return ap_[:, (h % 8) * 128:(h % 8 + 1) * 128], b_
            ap_, b_ = yt_seg[pp]
            for cc in range(2):
                r0 = cc * 64
                for h in range(H):
                    db = dbanks[h // 4]
                    oc = (h % 4) * 128
                    orows = db.ap[0:64, oc:oc + 128]
                    chap, chb = c0h_f(h)
                    clap, clb = c0l_f(h)
                    mmb(db, orows, qgv[:, h, r0:r0 + 64], chap, [bqtg_hi, chb], half="lo")
                    mmb(db, orows, qgv[:, h, r0:r0 + 64], clap, [bqtg_hi, clb], half="lo")
                    mmb(db, orows, qgl[:, h, r0:r0 + 64], chap, [bqtg_lo, chb], half="lo")
                if cc == 0:
                    v_tt(ap_[0:64, :], ap_[0:64, :], pA[0:64, :], AL.add,
                         [bank[0], bank[1], bank[2], bank[3], b_], [b_])
                else:
                    v_copy(ysc[0:64, :], pA[0:64, :],
                           [bank[0], bank[1], bank[2], bank[3]], [bysc])
                    dma("gpsimd", ysc[64:128, :], ysc[0:64, :], [bysc], [bysc])
                    v_tt(ap_[64:128, :], ap_[64:128, :], ysc[64:128, :], AL.add,
                         [bysc, b_], [b_])

        dma("sync", gamb[:, :], gam_in[:, :], [bconst], [bgam])
        dma("sync", betb[:, :], bet_in[:, :], [bconst], [bbet])
        for tb in range(4):
            yap, yb_ = yt_seg[tb]
            ln_stats_apply(yb_, yap, bysc, ysc[:, :], EPS_LN)
            v_tt(ysc[:, :], ysc[:, :], gamb[:, :], AL.mult, [bgam, bysc], [bysc])
            v_tt(ysc[:, :], ysc[:, :], betb[:, :], AL.add, [bbet, bysc], [bysc])
            for dc in range(16):
                pb = bank[6 + (dc % 2)]
                transp(ysc[:, dc * 128:(dc + 1) * 128], pb.ap[:, 0:128], pb, [bysc])
                col = dc * ET + tb * 128
                hseg, hb_ = seg(ynTh_seg, col, 128)
                v_copy(hseg, pb.ap[:, 0:128], [pb], [hb_])
                lseg, lb_ = seg(ynTl_seg, col, 128)
                v_stt(lseg, pb.ap[:, 0:128], 1.0, hseg, AL.mult, AL.subtract,
                      [pb, hb_], [lb_])

        for tbp in range(2):
            for dc in range(16):
                wh, bwh = (wrA_hi, bwrA_hi) if dc % 2 == 0 else (wrB_hi, bwrB_hi)
                wl, bwl = (wrA_lo, bwrA_lo) if dc % 2 == 0 else (wrB_lo, bwrB_lo)
                dma(WQ, wh[:, :], wsp_hi[2, dc * 128:(dc + 1) * 128, :], [bwsp], [bwh])
                dma(WQ, wl[:, :], wsp_lo[2, dc * 128:(dc + 1) * 128, :], [bwsp], [bwl])
                for ti in range(2):
                    tb = tbp * 2 + ti
                    col = dc * ET + tb * 128
                    yh, yhb = seg(ynTh_seg, col, 128)
                    yl, ylb = seg(ynTl_seg, col, 128)
                    for nt in range(4):
                        pb = bank[ti * 4 + nt]
                        mmb(pb, pb.ap[:, :], yh, wh[:, nt * 512:(nt + 1) * 512],
                            [yhb, bwh])
                        mmb(pb, pb.ap[:, :], yh, wl[:, nt * 512:(nt + 1) * 512],
                            [yhb, bwl])
                        mmb(pb, pb.ap[:, :], yl, wh[:, nt * 512:(nt + 1) * 512],
                            [ylb, bwh])
            for ti in range(2):
                tb = tbp * 2 + ti
                xap, xb_ = x1_seg[tb]
                dma("scalar", xap, x_in[tok0 + tb * 128:tok0 + tb * 128 + 128, :],
                    [bconst], [xb_])
                psrc = pA if ti == 0 else pB
                pbl = [bank[ti * 4 + i] for i in range(4)]
                v_tt(xap, xap, psrc[:, :], AL.add, pbl + [xb_], [xb_])

        for tb in range(4):
            xap, xb_ = x1_seg[tb]
            ln_stats_apply(xb_, xap, bysc, ysc[:, :], EPS_LN)
            for dc in range(16):
                pb = bank[6 + (dc % 2)]
                transp(ysc[:, dc * 128:(dc + 1) * 128], pb.ap[:, 0:128], pb, [bysc])
                col = dc * ET + tb * 128
                hseg, hb_ = seg(x2Th_seg, col, 128)
                v_copy(hseg, pb.ap[:, 0:128], [pb], [hb_])
                lseg, lb_ = seg(x2Tl_seg, col, 128)
                v_stt(lseg, pb.ap[:, 0:128], 1.0, hseg, AL.mult, AL.subtract,
                      [pb, hb_], [lb_])

        sqb = bank[5]
        for sc_ in range(4):
            for i16 in range(16):
                hb = sc_ * 16 + i16
                r = i16 % 2
                dma(WQ, fwg[r][:, :].rearrange("p (dc c) -> p dc c", dc=16),
                    wgu_eff[hb, :, :, :].transpose([1, 0, 2]), [bwgue], [bfwg[r]])
                dma(WQ, fwu[r][:, :].rearrange("p (dc c) -> p dc c", dc=16),
                    wgu_eff[64 + hb, :, :, :].transpose([1, 0, 2]), [bwgue], [bfwu[r]])
                gb_, ub_ = bank[6], bank[7]
                for dc in range(16):
                    xh, xhb = seg(x2Th_seg, dc * ET, 512)
                    xl, xlb = seg(x2Tl_seg, dc * ET, 512)
                    wg_ = fwg[r][:, dc * 128:(dc + 1) * 128]
                    wu_ = fwu[r][:, dc * 128:(dc + 1) * 128]
                    mmb(gb_, gb_.ap[:, :], wg_, xh, [bfwg[r], xhb])
                    mmb(gb_, gb_.ap[:, :], wg_, xl, [bfwg[r], xlb])
                    mmb(ub_, ub_.ap[:, :], wu_, xh, [bfwu[r], xhb])
                    mmb(ub_, ub_.ap[:, :], wu_, xl, [bfwu[r], xlb])
                s_act(hfb[:, :], gb_.ap[:, :], AF.Silu, [gb_], [bhf])
                v_tt(hfb[:, :], hfb[:, :], ub_.ap[:, :], AL.mult, [ub_, bhf], [bhf])
                hcol = i16 * 512
                hseg, hhb = seg(ynTh_seg, hcol, 512)
                v_copy(hseg, hfb[:, :], [bhf], [hhb])
                lseg, hlb = seg(ynTl_seg, hcol, 512)
                v_stt(lseg, hfb[:, :], 1.0, hseg, AL.mult, AL.subtract,
                      [bhf, hhb], [hlb])
                v_tt(h2b[:, :], hfb[:, :], hfb[:, :], AL.mult, [bhf], [bh2])
                mmb(sqb, sqb.ap[0:1, :], onesb[:, :], h2b[:, :], [bones, bh2], half="lo")
            for dt_ in range(4):
                for gidx in range(2):
                    dma(WQ, fwp[gidx][:, :].rearrange("p (hb c) -> p hb c", hb=4),
                        wp_eff[sc_ * 16 + gidx * 4:sc_ * 16 + (gidx + 1) * 4, dt_, :, :]
                        .transpose([1, 0, 2]), [bwpe], [bfwp[gidx]])
                for i4 in range(4):
                    rr = i4 % 2
                    if i4 >= 2:
                        dma(WQ, fwp[rr][:, :].rearrange("p (hb c) -> p hb c", hb=4),
                            wp_eff[sc_ * 16 + i4 * 4:sc_ * 16 + (i4 + 1) * 4, dt_, :, :]
                            .transpose([1, 0, 2]), [bwpe], [bfwp[rr]])
                    for ii in range(4):
                        i16 = i4 * 4 + ii
                        wp_ = fwp[rr][:, ii * 512:(ii + 1) * 512]
                        for tb in range(4):
                            ob = bank[tb]
                            hh_, hhb = seg(ynTh_seg, i16 * 512 + tb * 128, 128)
                            hl_, hlb = seg(ynTl_seg, i16 * 512 + tb * 128, 128)
                            mmb(ob, ob.ap[:, :], hh_, wp_, [hhb, bfwp[rr]])
                            mmb(ob, ob.ap[:, :], hl_, wp_, [hlb, bfwp[rr]])
                for tb in range(4):
                    ob = bank[tb]
                    yap, yb_ = yt_seg[tb]
                    dst = yap[:, dt_ * 512:(dt_ + 1) * 512]
                    if sc_ == 0:
                        v_copy(dst, ob.ap[:, :], [ob], [yb_])
                    else:
                        v_tt(dst, dst, ob.ap[:, :], AL.add, [ob, yb_], [yb_])
        v_copy(sqs[0:1, :], sqb.ap[0:1, :], [sqb], [bsqs])
        for tb in range(4):
            pb = bank[4]
            transp(sqs[0:1, tb * 128:(tb + 1) * 128], pb.ap[0:128, 0:1], pb, [bsqs])
            s_act(rinvt[:, tb:tb + 1], pb.ap[0:128, 0:1], AF.Sqrt, [pb], [brinv],
                  bias=float(EPS_RMS), scale=1.0 / HDIM)
        v_recip(rinvt[:, 4:8], rinvt[:, 0:4], [brinv], [brinv])
        for tb in range(4):
            yap, yb_ = yt_seg[tb]
            xap, xb_ = x1_seg[tb]
            v_stt(outb[:, :], yap, rinvt[:, 4 + tb:5 + tb], xap, AL.mult, AL.add,
                  [yb_, brinv, xb_], [boutb])
            dma("scalar", out_hbm[tok0 + tb * 128:tok0 + tb * 128 + 128, :],
                outb[:, :], [boutb], [bout])

    fin = [(snm, PROG.val[snm]) for snm in SEMNAMES if PROG.val[snm] > 0]

    def nop_fn(eng):
        return eng.engine_nop()
    PROG.op("gpsimd", nop_fn, fin, 1)

    with nc.Block() as block:
        sems = {}
        semstack = contextlib.ExitStack()
        for snm in SEMNAMES:
            sems[snm] = semstack.enter_context(nc.semaphore("sem_" + snm))

        def runner(eng_obj, ename):
            for fn, waits, inc, snm in PROG.q[ename]:
                for oe, ov in waits.items():
                    eng_obj.wait_ge(sems[oe], ov)
                inst = fn(eng_obj)
                inst.then_inc(sems[snm], inc)

        @block.tensor
        def _(eng):
            runner(eng, "tensor")

        @block.vector
        def _(eng):
            runner(eng, "vector")

        @block.scalar
        def _(eng):
            runner(eng, "scalar")

        @block.sync
        def _(eng):
            runner(eng, "sync")

        @block.gpsimd
        def _(eng):
            runner(eng, "gpsimd")

    stack.close()
    return nc


def _host_consts():
    bf = ml_dtypes.bfloat16
    ident = np.eye(128, dtype=np.float32)
    s_ = np.arange(64)
    u1 = (s_[:, None] <= s_[None, :]).astype(np.float32)
    U = np.zeros((128, 128), np.float32)
    U[0:64, 0:64] = u1
    U[64:128, 64:128] = u1
    M = U.copy()
    ag2 = np.zeros((128, 2), np.float32)
    ag2[0:64, 0] = 1.0
    ag2[64:128, 1] = 1.0
    agb = np.ascontiguousarray(ag2.T)
    ones = np.ones((128, 1), np.float32)
    return dict(identf=ident,
                ucum=np.ascontiguousarray(U.astype(bf)),
                maskbd=np.ascontiguousarray(M),
                ag2=np.ascontiguousarray(ag2.astype(bf)),
                agb=agb,
                onesb=np.ascontiguousarray(ones.astype(bf)))


def kernel(L0_attn_x, L0_attn_Wq, L0_attn_Wk, L0_attn_Wv, L0_attn_Wg, L0_attn_Wo,
           L0_attn_gamma, L0_attn_beta, L0_ffn_w_gate_up, L0_ffn_w_proj):
    x = np.ascontiguousarray(np.asarray(L0_attn_x, dtype=np.float32))
    ins_common = dict(
        wqT=np.ascontiguousarray(np.asarray(L0_attn_Wq, np.float32).T),
        wkT=np.ascontiguousarray(np.asarray(L0_attn_Wk, np.float32).T),
        wvT=np.ascontiguousarray(np.asarray(L0_attn_Wv, np.float32).T),
        wgT=np.ascontiguousarray(np.asarray(L0_attn_Wg, np.float32).T),
        woT=np.ascontiguousarray(np.asarray(L0_attn_Wo, np.float32).T),
        wguT=np.ascontiguousarray(np.asarray(L0_ffn_w_gate_up, np.float32).T),
        wpT=np.ascontiguousarray(np.asarray(L0_ffn_w_proj, np.float32).T),
        gammab=np.ascontiguousarray(
            np.broadcast_to(np.asarray(L0_attn_gamma, np.float32), (128, D))),
        betab=np.ascontiguousarray(
            np.broadcast_to(np.asarray(L0_attn_beta, np.float32), (128, D))),
    )
    ins_common.update(_host_consts())

    nc = build()
    in_maps = []
    for core in range(8):
        b, p = core // 4, core % 4
        cm = np.zeros((128, 4), np.float32)
        cm[:, 0:p] = 1.0
        m = dict(ins_common)
        m["x"] = np.ascontiguousarray(x[b, p * TPC:(p + 1) * TPC])
        m["cmask"] = cm
        in_maps.append(m)
    res = run_bass_kernel_spmd(nc, in_maps, core_ids=list(range(8)))
    out = np.zeros((2, 8192, D), np.float32)
    for core in range(8):
        b, p = core // 4, core % 4
        out[b, p * TPC:(p + 1) * TPC] = res.results[core]["out"]
    return out


# version 09
# speedup vs baseline: 1.0172x; 1.0172x over previous
import numpy as np
import ml_dtypes
import contextlib
import concourse.bass as bass
import concourse.mybir as mybir
from concourse.bass_utils import run_bass_kernel_spmd

F32 = mybir.dt.float32
BF16 = mybir.dt.bfloat16
U16 = mybir.dt.uint16
AF = mybir.ActivationFunctionType
AL = mybir.AluOpType
AX = mybir.AxisListType

H = 16
D = 2048
TPC = 2048
TT = 256
NT_ATT = TPC // TT
ET = 512
NT_EPI = TPC // ET
HDIM = 8192
GS = 64
CSCALE = float(128) ** -0.5
EPS_LN = 1e-5
EPS_RMS = 2.220446049250313e-16

ENGS = ("tensor", "vector", "scalar", "sync", "gpsimd")


NLANE = 8
DMAQ = ("sync", "gpsimd", "scalar")
SEMNAMES = list(ENGS) + ["%s_d%d" % (q, r) for q in DMAQ for r in range(NLANE)] + ["cc"]


class Prog:
    def __init__(self):
        self.q = {e: [] for e in ENGS}
        self.val = {s: 0 for s in SEMNAMES}
        self.waited = {e: {o: 0 for o in SEMNAMES} for e in ENGS}
        self.dma_cnt = {q: 0 for q in DMAQ}
        self.lane_last = {q: [None] * NLANE for q in DMAQ}

    def op(self, eng, emit_fn, deps=(), inc=1, sem=None):
        sem = sem or eng
        waits = {}
        for t in deps:
            if t is None:
                continue
            oe, ov = t
            if ov > self.waited[eng][oe]:
                waits[oe] = max(waits.get(oe, 0), ov)
        for oe, ov in waits.items():
            self.waited[eng][oe] = ov
        self.q[eng].append((emit_fn, waits, inc, sem))
        self.val[sem] += inc
        return (sem, self.val[sem])

    def dma_op(self, qeng, emit_fn, deps=()):
        r = self.dma_cnt[qeng] % NLANE
        self.dma_cnt[qeng] += 1
        lane = "%s_d%d" % (qeng, r)
        deps = list(deps)
        if self.lane_last[qeng][r] is not None:
            deps.append(self.lane_last[qeng][r])
        tok = self.op(qeng, emit_fn, deps, inc=16, sem=lane)
        self.lane_last[qeng][r] = tok
        return tok


class Buf:
    def __init__(self, ap=None):
        self.ap = ap
        self.w = None
        self.r = []
        self.epoch = {"lo": None, "hi": None}

    def close_epochs(self):
        if isinstance(self.epoch, dict):
            for h in ("lo", "hi"):
                if self.epoch[h] is not None:
                    self.epoch[h]["stop"] = True
                    self.epoch[h] = None


PROG = None


def emit(eng, fn, ins=(), outs=(), inc=1, dma=False, sem=None):
    deps = []
    for b in ins:
        deps.append(b.w)
        b.close_epochs()
    for b in outs:
        deps.extend(b.r)
        deps.append(b.w)
    if dma:
        tok = PROG.dma_op(eng, fn, deps)
    else:
        tok = PROG.op(eng, fn, deps, inc, sem=sem)
    for b in ins:
        b.r.append(tok)
    for b in outs:
        b.w = tok
        b.r = []
    return tok


def build(n_att_tiles=NT_ATT, n_epi_tiles=NT_EPI, skip_relay=False, n_quant=384, stage=99):
    global PROG
    PROG = Prog()
    nc = bass.Bass("TRN2", target_bir_lowering=False, debug=False)

    x_in = nc.dram_tensor("x", [TPC, D], F32, kind="ExternalInput").ap()
    w_ins = {}
    for nm, shp in (("wqT", [D, D]), ("wkT", [D, D]), ("wvT", [D, D]),
                    ("wgT", [D, D]), ("woT", [D, D]),
                    ("wguT", [D, 2 * HDIM]), ("wpT", [HDIM, D])):
        w_ins[nm] = nc.dram_tensor(nm, shp, F32, kind="ExternalInput").ap()
    gam_in = nc.dram_tensor("gammab", [128, D], F32, kind="ExternalInput").ap()
    bet_in = nc.dram_tensor("betab", [128, D], F32, kind="ExternalInput").ap()
    cmask_in = nc.dram_tensor("cmask", [128, 4], F32, kind="ExternalInput").ap()
    ident_in = nc.dram_tensor("identf", [128, 128], F32, kind="ExternalInput").ap()
    ucum_in = nc.dram_tensor("ucum", [128, 128], BF16, kind="ExternalInput").ap()
    mask_in = nc.dram_tensor("maskbd", [128, 128], F32, kind="ExternalInput").ap()
    ag2_in = nc.dram_tensor("ag2", [128, 2], BF16, kind="ExternalInput").ap()
    agb_in = nc.dram_tensor("agb", [2, 128], F32, kind="ExternalInput").ap()
    ones_in = nc.dram_tensor("onesb", [128, 1], BF16, kind="ExternalInput").ap()
    out_hbm = nc.dram_tensor("out", [TPC, D], F32, kind="ExternalOutput").ap()

    wsp_hi = nc.dram_tensor("wsp_hi", [3, D, D], BF16).ap()
    wsp_lo = nc.dram_tensor("wsp_lo", [3, D, D], BF16).ap()
    wqk_hi = nc.dram_tensor("wqk_hi", [2, H, 128, 16 * 128], BF16).ap()
    wqk_lo = nc.dram_tensor("wqk_lo", [2, H, 128, 16 * 128], BF16).ap()
    y_sp = nc.dram_tensor("y_sp", [TPC, D], F32).ap()
    qth_sp = nc.dram_tensor("qth_sp", [TPC // 128, 128, D], BF16).ap()
    qtl_sp = nc.dram_tensor("qtl_sp", [TPC // 128, 128, D], BF16).ap()
    wgu_eff = nc.dram_tensor("wgu_eff", [128, 128, 16, 128], BF16).ap()
    wp_eff = nc.dram_tensor("wp_eff", [64, 4, 128, 512], BF16).ap()
    relay_in = nc.dram_tensor("relay_in", [128, D + 16], F32).ap()
    relay_out = nc.dram_tensor("relay_out", [4 * 128, D + 16], F32).ap()

    stack = contextlib.ExitStack()

    def sb(name, p, cols, dt):
        t = stack.enter_context(nc.sbuf_tensor(name, [p, cols], dt))
        return t, Buf(t)

    identf, bident = sb("identf_s", 128, 128, F32)
    ucum, bucum = sb("ucum_s", 128, 128, BF16)
    maskbd, bmask = sb("maskbd_s", 128, 128, F32)
    ag2, bag2 = sb("ag2_s", 128, 2, BF16)
    agb, bagb = sb("agb_s", 2, 128, F32)
    onesb, bones = sb("onesb_s", 128, 1, BF16)
    cmaskb, bcm = sb("cmaskb", 128, 4, F32)

    x2, bx2 = sb("x2", 128, D, F32)
    xnb, bxn = sb("xnb", 128, D, F32)
    xnT_hi, bxnT_hi = sb("xnT_hi", 128, 16 * TT, BF16)
    xnT_lo, bxnT_lo = sb("xnT_lo", 128, 16 * TT, BF16)
    qf, bqf = sb("qf", 128, H * TT, F32)
    kf, bkf = sb("kf", 128, H * TT, F32)
    e2b, be2 = sb("e2b", 128, D, F32)
    gsp, bgsp = sb("gsp", 128, D, F32)
    g_hi, bg_hi = sb("g_hi", 128, D, BF16)
    g_lo, bg_lo = sb("g_lo", 128, D, BF16)
    v_hi, bv_hi = sb("v_hi", 128, D, BF16)
    v_lo, bv_lo = sb("v_lo", 128, D, BF16)
    Sst, bS = sb("Sst", 128, D, F32)
    ktf, bktf = sb("ktf", 128, D, F32)
    khf, bkhf = sb("khf", 128, D, F32)
    kt_hi, bkt_hi = sb("kt_hi", 128, D, BF16)
    kt_lo, bkt_lo = sb("kt_lo", 128, D, BF16)
    kht_hi, bkht_hi = sb("kht_hi", 128, D, BF16)
    kht_lo, bkht_lo = sb("kht_lo", 128, D, BF16)
    qt_hi, bqt_hi = sb("qt_hi", 128, D, BF16)
    qt_lo, bqt_lo = sb("qt_lo", 128, D, BF16)
    pm_hi, bpm_hi = sb("pm_hi", 128, D, BF16)
    pm_lo, bpm_lo = sb("pm_lo", 128, D, BF16)
    st_hi, bst_hi = sb("st_hi", 128, D, BF16)
    st_lo, bst_lo = sb("st_lo", 128, D, BF16)
    wrA_hi, bwrA_hi = sb("wrA_hi", 128, D, BF16)
    wrA_lo, bwrA_lo = sb("wrA_lo", 128, D, BF16)
    wrB_hi, bwrB_hi = sb("wrB_hi", 128, D, BF16)
    wrB_lo, bwrB_lo = sb("wrB_lo", 128, D, BF16)
    nb31, bnb31 = sb("nb31", 128, 32, F32)
    nb63, bnb63 = sb("nb63", 128, 32, F32)
    betlog, bbetlog = sb("betlog", 128, 32 * 16, F32)
    dlog, bdlog = sb("dlog", 128, 32 * 16, F32)
    smalls, bsm = sb("smalls", 128, 64, F32)
    qw, bqw = sb("qw", 128, 1024, F32)
    qA, bqA = sb("qA", 128, 1024, BF16)
    qB, bqB = sb("qB", 128, 1024, BF16)
    qC, bqC = sb("qC", 128, 1024, BF16)
    qD, bqD = sb("qD", 128, 1024, BF16)
    qsci, bqsci = sb("qsci", 2, 1024, F32)
    qscb, bqscb = sb("qscb", 2, 1024, BF16)
    agbb, bagbb = sb("agbb", 2, 128, BF16)
    qA2, bqA2 = sb("qA2", 128, 1024, BF16)
    qB2, bqB2 = sb("qB2", 128, 1024, BF16)
    rly_d, brly_d = sb("rly_d", 128, 16, F32)
    fexp, bfexp = sb("fexp", 128, 16, F32)
    fsum, bfsum = sb("fsum", 128, 16, F32)
    gcur, bgcur = sb("gcur", 128, 16, F32)
    egam, begam = sb("egam", 128, 32, F32)
    rinvt, brinv = sb("rinvt", 128, 8, F32)
    sqs, bsqs = sb("sqs", 1, ET, F32)

    cvals, bcv = sb("cvals", 128, 8, F32)

    pA = stack.enter_context(nc.psum_tensor("pA", [128, 2048], F32))
    pB = stack.enter_context(nc.psum_tensor("pB", [128, 2048], F32))
    bank = [Buf(pA[:, i * 512:(i + 1) * 512]) for i in range(4)] + \
           [Buf(pB[:, i * 512:(i + 1) * 512]) for i in range(4)]

    bconst = Buf(None)
    bwsp = Buf(None); bwqk = Buf(None); bysp = Buf(None)
    bqth = Buf(None); bqtl = Buf(None)
    bwgue = Buf(None); bwpe = Buf(None)
    brin = Buf(None); brout = Buf(None); bout = Buf(None)

    yt_seg = [(qf[:, 0:D], bqf), (qf[:, D:2 * D], bqf),
              (kf[:, 0:D], bkf), (kf[:, D:2 * D], bkf)]
    x1_seg = [(e2b[:, :], be2), (gsp[:, :], bgsp), (Sst[:, :], bS), (xnb[:, :], bxn)]
    ynTh_seg = [(xnT_hi[:, i * 512:(i + 1) * 512], bxnT_hi) for i in range(8)] + \
               [(xnT_lo[:, i * 512:(i + 1) * 512], bxnT_lo) for i in range(8)]
    ynTl_seg = [(v_hi[:, i * 512:(i + 1) * 512], bv_hi) for i in range(4)] + \
               [(v_lo[:, i * 512:(i + 1) * 512], bv_lo) for i in range(4)] + \
               [(g_hi[:, i * 512:(i + 1) * 512], bg_hi) for i in range(4)] + \
               [(g_lo[:, i * 512:(i + 1) * 512], bg_lo) for i in range(4)]
    x2Th_seg = [(kt_hi[:, i * 512:(i + 1) * 512], bkt_hi) for i in range(4)] + \
               [(kt_lo[:, i * 512:(i + 1) * 512], bkt_lo) for i in range(4)] + \
               [(qt_hi[:, i * 512:(i + 1) * 512], bqt_hi) for i in range(4)] + \
               [(qt_lo[:, i * 512:(i + 1) * 512], bqt_lo) for i in range(4)]
    x2Tl_seg = [(kht_hi[:, i * 512:(i + 1) * 512], bkht_hi) for i in range(4)] + \
               [(kht_lo[:, i * 512:(i + 1) * 512], bkht_lo) for i in range(4)] + \
               [(pm_hi[:, i * 512:(i + 1) * 512], bpm_hi) for i in range(4)] + \
               [(pm_lo[:, i * 512:(i + 1) * 512], bpm_lo) for i in range(4)]

    def seg(lst, col, w):
        s, off = divmod(col, 512)
        assert off + w <= 512 or w == 512 and off == 0
        ap_, b_ = lst[s]
        return ap_[:, off:off + w], b_

    gamb, bgam = ktf, bktf
    betb, bbet = khf, bkhf
    qtg_hi, bqtg_hi = st_hi, bst_hi
    qtg_lo, bqtg_lo = st_lo, bst_lo
    sin_b, bsin = x2, bx2
    rly_t, brly_t = khf, bkhf
    ysc, bysc = x2, bx2
    outb, boutb = x2, bx2
    hfb, bhf = qw[:, 0:512], bqw
    h2b, bh2 = qA[:, 0:512], bqA
    ybuf, bybuf = gsp, bgsp
    pmf, bpmf = ktf, bktf
    qtf, bqtf = khf, bkhf
    stfv, bstf = xnb, bxn
    fwg = [wrA_hi, wrA_lo]; bfwg = [bwrA_hi, bwrA_lo]
    fwu = [wrB_hi, wrB_lo]; bfwu = [bwrB_hi, bwrB_lo]
    fwp = [qw[:, :].bitcast(BF16), khf[:, :].bitcast(BF16)[:, 0:2048]]
    bfwp = [bqw, bkhf]
    c0_hi_seg = [(qA[:, :], bqA), (qB[:, :], bqB)]
    c0_lo_seg = [(qC[:, :], bqC), (qD[:, :], bqD)]

    def dma(queue, dst_ap, src_ap, ins, outs):
        def fn(eng):
            return eng.dma_start(out=dst_ap, in_=src_ap)
        return emit(queue, fn, ins, outs, dma=True)

    def mmb(bankbuf, out_ap, lhsT, rhs, in_bufs, tp=None, half="both"):
        halves = ("lo", "hi") if half == "both" else (half,)
        rec = {"start": all(bankbuf.epoch[h] is None for h in halves), "stop": False}
        for h in halves:
            bankbuf.epoch[h] = rec

        def fn(eng):
            return eng.matmul(out_ap, lhsT, rhs, start=rec["start"], stop=rec["stop"],
                              tile_position=tp, skip_group_check=True)
        return emit("tensor", fn, in_bufs, [bankbuf], inc=1)

    def transp(in_ap, out_ap, bankbuf, in_bufs, tp=None):
        bankbuf.close_epochs()
        n = in_ap.shape[0]

        def fn(eng):
            return eng.transpose(out_ap, in_ap, identf[0:n, 0:n], tile_position=tp)
        return emit("tensor", fn, list(in_bufs) + [bident], [bankbuf], inc=1)

    def v_tt(out_ap, a_ap, b_ap, op, ins, outs, eng="vector"):
        def fn(e):
            return e.tensor_tensor(out_ap, a_ap, b_ap, op)
        return emit(eng, fn, ins, outs)

    def v_ts(out_ap, a_ap, s1, ins, outs, op0=AL.mult, s2=None, op1=AL.bypass, eng="vector"):
        def fn(e):
            return e.tensor_scalar(out_ap, a_ap, s1, s2, op0, op1)
        return emit(eng, fn, ins, outs)

    def v_stt(out_ap, a_ap, s, b_ap, op0, op1, ins, outs, eng="vector"):
        def fn(e):
            return e.scalar_tensor_tensor(out_ap, a_ap, s, b_ap, op0, op1)
        return emit(eng, fn, ins, outs)

    def v_copy(out_ap, a_ap, ins, outs, eng="vector"):
        def fn(e):
            return e.tensor_copy(out_ap, a_ap)
        return emit(eng, fn, ins, outs)

    def v_red(out_ap, a_ap, op, ins, outs):
        def fn(e):
            return e.tensor_reduce(out_ap, a_ap, AX.X, op)
        return emit("vector", fn, ins, outs)

    def v_recip(out_ap, a_ap, ins, outs):
        def fn(e):
            return e.reciprocal(out_ap, a_ap)
        return emit("vector", fn, ins, outs)

    def s_act(out_ap, a_ap, func, ins, outs, bias=0.0, scale=1.0, accum=None):
        def fn(e):
            return e.activation(out_ap, a_ap, func, bias=bias, scale=scale, accum_out=accum)
        return emit("scalar", fn, list(ins) + [bcv], outs)

    def s_mul(out_ap, a_ap, m, ins, outs):
        def fn(e):
            return e.mul(out_ap, a_ap, m)
        return emit("scalar", fn, list(ins) + [bcv], outs)

    def v_memset(ap_, val, outs, eng="vector"):
        def fn(e):
            return e.memset(ap_, val)
        return emit(eng, fn, (), outs)

    _cv = [float(EPS_LN), 1.0, float(EPS_RMS), -1.0, 1.0 / HDIM]
    for i, v in enumerate(_cv):
        def mkset(ap_, val):
            def fn(e):
                return e.memset(ap_, val)
            return fn
        emit("vector", mkset(cvals[:, i:i + 1], v), (), [bcv])
        nc.const_aps.register(F32, v, cvals[:, i:i + 1])

    for dst, bdst, src in ((identf, bident, ident_in), (ucum, bucum, ucum_in),
                           (maskbd, bmask, mask_in), (ag2, bag2, ag2_in),
                           (agb, bagb, agb_in), (onesb, bones, ones_in),
                           (cmaskb, bcm, cmask_in)):
        dma("sync", dst[:, :], src[:, :], [bconst], [bdst])
    v_copy(agbb[0:2, :], agb[0:2, :], [bagb], [bagbb])
    v_memset(Sst[:, :], 0.0, [bS])
    v_memset(dlog[:, :], 0.0, [bdlog])
    v_memset(betlog[:, :], 0.0, [bbetlog])

    _p1i = 0
    for (nm, cs, mode, mi) in (("wqT", CSCALE, "qk", 0), ("wkT", 1.0, "qk", 1),
                               ("wgT", 1.0, "row", 0), ("wvT", 1.0, "row", 1),
                               ("woT", 1.0, "row", 2)):
        wap = w_ins[nm]
        for dc in range(16):
            if _p1i % 2 == 0:
                Px, Pbx, Ph, Pbh, Pl, Pbl = x2, bx2, wrA_hi, bwrA_hi, wrA_lo, bwrA_lo
            else:
                Px, Pbx, Ph, Pbh, Pl, Pbl = xnb, bxn, wrB_hi, bwrB_hi, wrB_lo, bwrB_lo
            inq = ("sync", "scalar")[_p1i % 2]
            outq = ("gpsimd", "sync", "scalar")[_p1i % 3]
            _p1i += 1
            dma(inq, Px[:, :], wap[dc * 128:(dc + 1) * 128, :], [bconst], [Pbx])
            v_ts(Ph[:, :], Px[:, :], float(cs), [Pbx], [Pbh])
            v_stt(Pl[:, :], Px[:, :], float(cs), Ph[:, :], AL.mult, AL.subtract,
                  [Pbx, Pbh], [Pbl])
            if mode == "row":
                dma(outq, wsp_hi[mi, dc * 128:(dc + 1) * 128, :], Ph[:, :],
                    [Pbh], [bwsp])
                dma(outq, wsp_lo[mi, dc * 128:(dc + 1) * 128, :], Pl[:, :],
                    [Pbl], [bwsp])
            else:
                dma(outq, wqk_hi[mi, :, :, dc * 128:(dc + 1) * 128].transpose([1, 0, 2]),
                    Ph[:, :].rearrange("p (h c) -> p h c", h=16), [Pbh], [bwqk])
                dma(outq, wqk_lo[mi, :, :, dc * 128:(dc + 1) * 128].transpose([1, 0, 2]),
                    Pl[:, :].rearrange("p (h c) -> p h c", h=16), [Pbl], [bwqk])

    qstate = {"i": 0}

    def emit_quant_tile(idx):
        if idx < 256:
            rc, ct = idx // 16, idx % 16
            src = w_ins["wguT"][rc * 128:(rc + 1) * 128, ct * 1024:(ct + 1) * 1024]
            is_gu = True
        else:
            t_ = idx - 256
            rc, ct = t_ // 2, t_ % 2
            src = w_ins["wpT"][rc * 128:(rc + 1) * 128, ct * 1024:(ct + 1) * 1024]
            is_gu = False
        if idx % 2 == 0:
            LqA, LbqA, LqB, LbqB = qA, bqA, qB, bqB
        else:
            LqA, LbqA, LqB, LbqB = qA2, bqA2, qB2, bqB2
        Lqw, Lbqw = qw, bqw
        LqC, LbqC, LqD, LbqD = qC, bqC, qD, bqD
        Lqsci, Lbqsci, Lqscb, Lbqscb = qsci, bqsci, qscb, bqscb
        dma("gpsimd", Lqw[:, :], src, [bconst], [Lbqw])
        v_copy(LqA[:, :], Lqw[:, :], [Lbqw], [LbqA])
        s_act(LqB[:, :], LqA[:, :], AF.Abs, [LbqA], [LbqB])
        pbuf = bank[7]
        sclb = bank[3]
        for s4 in range(2):
            sl = slice(s4 * 512, (s4 + 1) * 512)
            mmb(pbuf, pbuf.ap[0:2, :], ag2[:, :], LqB[:, sl],
                [bag2, LbqB], half="lo")
            v_ts(Lqscb[0:2, sl], pbuf.ap[0:2, :], 1.0 / GS, [pbuf], [Lbqscb], op0=AL.mult,
                 s2=float(np.float32(ml_dtypes.bfloat16(1e-8))), op1=AL.max)
            v_recip(Lqsci[0:2, sl], Lqscb[0:2, sl], [Lbqscb], [Lbqsci])
        for s4 in range(2):
            sl = slice(s4 * 512, (s4 + 1) * 512)
            mmb(pbuf, pbuf.ap[:, :], agb[0:2, :], Lqsci[0:2, sl],
                [bagb, Lbqsci])
            v_tt(LqB[:, sl], LqA[:, sl], pbuf.ap[:, :], AL.mult,
                 [LbqA, pbuf], [LbqB])
        v_ts(LqC[:, :], LqB[:, :], -0.5, [LbqB], [LbqC], op0=AL.is_lt)
        v_stt(LqD[:, :], LqB[:, :], 0.5, LqC[:, :], AL.is_gt, AL.subtract,
              [LbqB, LbqC], [LbqD])
        for s4 in range(2):
            sl = slice(s4 * 512, (s4 + 1) * 512)
            mmb(sclb, sclb.ap[:, :], agbb[0:2, :], Lqscb[0:2, sl],
                [bagbb, Lbqscb])
            v_tt(LqB[:, sl], LqD[:, sl], sclb.ap[:, :], AL.mult,
                 [LbqD, sclb], [LbqB])
        v_tt(LqC[:, :], LqB[:, :], LqA[:, :], AL.subtract, [LbqB, LbqA], [LbqC], eng="gpsimd")
        v_tt(LqB[:, :], LqA[:, :], LqC[:, :], AL.add, [LbqA, LbqC], [LbqB], eng="gpsimd")
        if is_gu:
            hb0 = ct * 8
            dst = wgu_eff[hb0:hb0 + 8, :, rc, :].transpose([1, 0, 2])
            dma("gpsimd", dst.bitcast(U16),
                LqB[:, :].rearrange("p (hb c) -> p hb c", hb=8).bitcast(U16),
                [LbqB], [bwgue])
        else:
            dc0 = ct * 2
            dst = wp_eff[rc, dc0:dc0 + 2, :, :].transpose([1, 0, 2])
            dma("gpsimd", dst.bitcast(U16),
                LqB[:, :].rearrange("p (dc c) -> p dc c", dc=2).bitcast(U16),
                [LbqB], [bwpe])

    def drip_quant(n):
        for _ in range(n):
            if qstate["i"] < n_quant:
                emit_quant_tile(qstate["i"])
                qstate["i"] += 1

    quant_per_tile = (n_quant + n_att_tiles - 1) // max(1, n_att_tiles)

    def ln_stats_apply(src_buf, src_ap, dst_buf, dst_ap, eps, dump=None):
        v_red(smalls[:, 0:1], src_ap, AL.add, [src_buf], [bsm])
        for bb in (bank[0], bank[1], bank[2], bank[3]):
            bb.close_epochs()
        s_act(pA[:, :], src_ap, AF.Square,
              [src_buf], [bank[0], bank[1], bank[2], bank[3], bsm],
              accum=smalls[:, 1:2])
        v_ts(smalls[:, 2:3], smalls[:, 0:1], 1.0 / D, [bsm], [bsm])
        v_ts(smalls[:, 3:4], smalls[:, 1:2], 1.0 / D, [bsm], [bsm])
        v_tt(smalls[:, 4:5], smalls[:, 2:3], smalls[:, 2:3], AL.mult, [bsm], [bsm])
        v_tt(smalls[:, 5:6], smalls[:, 3:4], smalls[:, 4:5], AL.subtract, [bsm], [bsm])
        s_act(smalls[:, 6:7], smalls[:, 5:6], AF.Sqrt, [bsm], [bsm], bias=float(eps))
        v_recip(smalls[:, 7:8], smalls[:, 6:7], [bsm], [bsm])
        v_stt(smalls[:, 8:9], smalls[:, 2:3], -1.0, smalls[:, 7:8], AL.mult, AL.mult,
              [bsm], [bsm])
        s_act(dst_ap, src_ap, AF.Identity, [src_buf, bsm], [dst_buf],
              bias=smalls[:, 8:9], scale=smalls[:, 7:8])

    qfv = qf.rearrange("p (h t) -> p h t", h=H)
    kfv = kf.rearrange("p (h t) -> p h t", h=H)
    e2v = e2b.rearrange("p (h t) -> p h t", h=H)
    sv = Sst.rearrange("p (h t) -> p h t", h=H)
    WQ = "sync"
    _wqc = {"i": 0}

    def wq_next():
        q = ("sync", "scalar")[_wqc["i"] % 2]
        _wqc["i"] += 1
        return q

    for t in range(n_att_tiles):
        for tb in range(2):
            pair = t * 2 + tb
            dma("scalar", x2[:, :], x_in[pair * 128:(pair + 1) * 128, :], [bconst], [bx2])
            ln_stats_apply(bx2, x2[:, :], bxn, xnb[:, :], EPS_LN)
            for dc in range(16):
                pb = bank[dc % 2]
                transp(xnb[:, dc * 128:(dc + 1) * 128], pb.ap[:, 0:128], pb, [bxn])
                col = dc * TT + tb * 128
                v_copy(xnT_hi[:, col:col + 128], pb.ap[:, 0:128], [pb], [bxnT_hi])
                v_stt(xnT_lo[:, col:col + 128], pb.ap[:, 0:128], 1.0,
                      xnT_hi[:, col:col + 128], AL.mult, AL.subtract,
                      [pb, bxnT_hi], [bxnT_lo])

        for half in range(2 if stage >= 2 else 0):
            for hh in range(8):
                h = half * 8 + hh
                dma(wq_next(), wrA_hi[:, :], wqk_hi[0, h, :, :], [bwqk], [bwrA_hi])
                dma(wq_next(), wrA_lo[:, :], wqk_lo[0, h, :, :], [bwqk], [bwrA_lo])
                dma(wq_next(), wrB_hi[:, :], wqk_hi[1, h, :, :], [bwqk], [bwrB_hi])
                dma(wq_next(), wrB_lo[:, :], wqk_lo[1, h, :, :], [bwqk], [bwrB_lo])
                qbi = (3, 0, 1, 2)[hh // 2]
                kbi = (7, 4, 5, 6)[hh // 2]
                qb = bank[qbi]
                kb = bank[kbi]
                oc = (hh % 2) * 256
                for dc in range(16):
                    xh = xnT_hi[:, dc * TT:(dc + 1) * TT]
                    xl = xnT_lo[:, dc * TT:(dc + 1) * TT]
                    w0 = wrA_hi[:, dc * 128:(dc + 1) * 128]
                    w1 = wrA_lo[:, dc * 128:(dc + 1) * 128]
                    mmb(qb, qb.ap[:, oc:oc + 256], w0, xh, [bwrA_hi, bxnT_hi])
                    mmb(qb, qb.ap[:, oc:oc + 256], w0, xl, [bwrA_hi, bxnT_lo])
                    mmb(qb, qb.ap[:, oc:oc + 256], w1, xh, [bwrA_lo, bxnT_hi])
                for dc in range(16):
                    xh = xnT_hi[:, dc * TT:(dc + 1) * TT]
                    xl = xnT_lo[:, dc * TT:(dc + 1) * TT]
                    w0 = wrB_hi[:, dc * 128:(dc + 1) * 128]
                    w1 = wrB_lo[:, dc * 128:(dc + 1) * 128]
                    mmb(kb, kb.ap[:, oc:oc + 256], w0, xh, [bwrB_hi, bxnT_hi])
                    mmb(kb, kb.ap[:, oc:oc + 256], w0, xl, [bwrB_hi, bxnT_lo])
                    mmb(kb, kb.ap[:, oc:oc + 256], w1, xh, [bwrB_lo, bxnT_hi])
                if hh % 2 == 1:
                    for hh2 in (hh - 1, hh):
                        h2 = half * 8 + hh2
                        oc2 = (hh2 % 2) * 256
                        v_copy(qfv[:, h2, :], bank[qbi].ap[:, oc2:oc2 + 256],
                               [bank[qbi]], [bqf])
                        v_copy(kfv[:, h2, :], bank[kbi].ap[:, oc2:oc2 + 256],
                               [bank[kbi]], [bkf])
                if hh == 1:
                    drip_quant(quant_per_tile // 4)

        for tb in range(2 if stage >= 3 else 0):
            pair = t * 2 + tb

            for dc in range(16):
                wh, bwh = (wrA_hi, bwrA_hi) if dc % 2 == 0 else (wrB_hi, bwrB_hi)
                wl, bwl = (wrA_lo, bwrA_lo) if dc % 2 == 0 else (wrB_lo, bwrB_lo)
                dma(wq_next(), wh[:, :], wsp_hi[0, dc * 128:(dc + 1) * 128, :], [bwsp], [bwh])
                dma(wq_next(), wl[:, :], wsp_lo[0, dc * 128:(dc + 1) * 128, :], [bwsp], [bwl])
                col = dc * TT + tb * 128
                xh = xnT_hi[:, col:col + 128]
                xl = xnT_lo[:, col:col + 128]
                if stage < 3.005:
                    continue
                for nt in range(4):
                    pb = bank[4 + nt]
                    mmb(pb, pb.ap[:, :], xh, wh[:, nt * 512:(nt + 1) * 512], [bxnT_hi, bwh])
                    mmb(pb, pb.ap[:, :], xh, wl[:, nt * 512:(nt + 1) * 512], [bxnT_hi, bwl])
                    mmb(pb, pb.ap[:, :], xl, wh[:, nt * 512:(nt + 1) * 512], [bxnT_lo, bwh])
            for nt in range(4 if stage >= 3.01 else 0):
                pb = bank[4 + nt]
                sl = slice(nt * 512, (nt + 1) * 512)
                v_ts(gsp[:, sl], pb.ap[:, :], 0.0, [pb], [bgsp], op0=AL.min)
                if stage < 3.02:
                    continue
                v_stt(xnb[:, sl], gsp[:, sl], 2.0, pb.ap[:, :], AL.mult, AL.subtract,
                      [bgsp, pb], [bxn])
            if stage >= 3.04:
                s_act(xnb[:, :], xnb[:, :], AF.Exp, [bxn], [bxn])
            if stage >= 3.05:
                s_act(xnb[:, :], xnb[:, :], AF.Ln, [bxn], [bxn], bias=1.0)
            if stage >= 3.06:
                v_tt(gsp[:, :], gsp[:, :], xnb[:, :], AL.subtract, [bgsp, bxn], [bgsp])
            if stage < 3.2:
                continue
            v_copy(g_hi[:, :], gsp[:, :], [bgsp], [bg_hi])
            v_stt(g_lo[:, :], gsp[:, :], 1.0, g_hi[:, :], AL.mult, AL.subtract,
                  [bgsp, bg_hi], [bg_lo])

            for grp in range(2):
                pbs = [bank[4 + 2 * grp], bank[5 + 2 * grp]]
                for hh in range(8):
                    h = grp * 8 + hh
                    pb = pbs[hh // 4]
                    oc = (hh % 4) * 128
                    mmb(pb, pb.ap[:, oc:oc + 128], g_hi[:, h * 128:(h + 1) * 128],
                        ucum[:, :], [bg_hi, bucum])
                    mmb(pb, pb.ap[:, oc:oc + 128], g_lo[:, h * 128:(h + 1) * 128],
                        ucum[:, :], [bg_lo, bucum])
                for hh in range(8 if stage >= 3.5 else 0):
                    h = grp * 8 + hh
                    pb = pbs[hh // 4]
                    oc = (hh % 4) * 128
                    for cc in range(2):
                        gj = pair * 2 + cc
                        c31 = oc + cc * 64 + 31
                        c63 = oc + cc * 64 + 63
                        v_ts(nb31[:, cc * 16 + h:cc * 16 + h + 1], pb.ap[:, c31:c31 + 1],
                             -1.0, [pb], [bnb31])
                        v_ts(nb63[:, cc * 16 + h:cc * 16 + h + 1], pb.ap[:, c63:c63 + 1],
                             -1.0, [pb], [bnb63])
                        v_ts(betlog[:, gj * 16 + h:gj * 16 + h + 1], pb.ap[:, c31:c31 + 1],
                             1.0, [pb], [bbetlog])
                        v_ts(dlog[:, gj * 16 + h:gj * 16 + h + 1], pb.ap[:, c63:c63 + 1],
                             1.0, [pb], [bdlog])
                for hh in range(8):
                    h = grp * 8 + hh
                    pb = pbs[hh // 4]
                    oc = (hh % 4) * 128
                    for cc in range(2):
                        s_act(e2v[:, h, cc * 64:(cc + 1) * 64],
                              pb.ap[:, oc + cc * 64:oc + cc * 64 + 64], AF.Exp,
                              [pb, bnb31], [be2],
                              bias=nb31[:, cc * 16 + h:cc * 16 + h + 1])

            for dc in range(16 if stage >= 3.8 else 0):
                wh, bwh = (wrA_hi, bwrA_hi) if dc % 2 == 0 else (wrB_hi, bwrB_hi)
                wl, bwl = (wrA_lo, bwrA_lo) if dc % 2 == 0 else (wrB_lo, bwrB_lo)
                dma(wq_next(), wh[:, :], wsp_hi[1, dc * 128:(dc + 1) * 128, :], [bwsp], [bwh])
                dma(wq_next(), wl[:, :], wsp_lo[1, dc * 128:(dc + 1) * 128, :], [bwsp], [bwl])
                col = dc * TT + tb * 128
                xh = xnT_hi[:, col:col + 128]
                xl = xnT_lo[:, col:col + 128]
                for nt in range(4):
                    pb = bank[4 + nt]
                    mmb(pb, pb.ap[:, :], xh, wh[:, nt * 512:(nt + 1) * 512], [bxnT_hi, bwh])
                    mmb(pb, pb.ap[:, :], xh, wl[:, nt * 512:(nt + 1) * 512], [bxnT_hi, bwl])
                    mmb(pb, pb.ap[:, :], xl, wh[:, nt * 512:(nt + 1) * 512], [bxnT_lo, bwh])
            for nt in range(4):
                pb = bank[4 + nt]
                sl = slice(nt * 512, (nt + 1) * 512)
                v_copy(v_hi[:, sl], pb.ap[:, :], [pb], [bv_hi])
                v_stt(v_lo[:, sl], pb.ap[:, :], 1.0, v_hi[:, sl], AL.mult, AL.subtract,
                      [pb, bv_hi], [bv_lo])

            if stage < 4:
                continue
            ktv3 = ktf.rearrange("p (h t) -> p h t", h=H)
            e1ap = x2.rearrange("p (h t) -> p h t", h=H)
            v_recip(e1ap, e2v, [be2], [bx2])
            v_tt(ktv3, kfv[:, :, tb * 128:(tb + 1) * 128], e1ap, AL.mult,
                 [bkf, bx2], [bktf])
            v_copy(kt_hi[:, :], ktf[:, :], [bktf], [bkt_hi])
            v_stt(kt_lo[:, :], ktf[:, :], 1.0, kt_hi[:, :], AL.mult, AL.subtract,
                  [bktf, bkt_hi], [bkt_lo])
            qtv3 = qtf.rearrange("p (h t) -> p h t", h=H)
            v_tt(qtv3, qfv[:, :, tb * 128:(tb + 1) * 128], e2v, AL.mult,
                 [bqf, be2], [bqtf])
            v_copy(qt_hi[:, :], qtf[:, :], [bqtf], [bqt_hi])
            v_stt(qt_lo[:, :], qtf[:, :], 1.0, qt_hi[:, :], AL.mult, AL.subtract,
                  [bqtf, bqt_hi], [bqt_lo])
            dma("gpsimd", qth_sp[pair, :, :], qt_hi[:, :], [bqt_hi], [bqth])
            dma("gpsimd", qtl_sp[pair, :, :], qt_lo[:, :], [bqt_lo], [bqtl])
            v_tt(smalls[:, 16:48], nb31[:, 0:32], nb63[:, 0:32], AL.subtract,
                 [bnb31, bnb63], [bsm])
            s_act(smalls[:, 16:48], smalls[:, 16:48], AF.Exp, [bsm], [bsm])
            e4v = smalls[:, 16:48].rearrange("p (c h) -> p c h", c=2).transpose([0, 2, 1])
            e4bc = e4v.unsqueeze(3).broadcast_to([128, H, 2, 64])
            v_tt(khf.rearrange("p (h c t) -> p h c t", h=H, c=2),
                 ktf.rearrange("p (h c t) -> p h c t", h=H, c=2), e4bc, AL.mult,
                 [bktf, bsm], [bkhf])
            khv = khf.rearrange("p (h t) -> p h t", h=H)
            for h in range(H):
                pb = bank[h % 2]
                transp(khv[:, h, 0:64], pb.ap[0:64, 0:128], pb, [bkhf])
                v_copy(kht_hi[0:64, h * 128:(h + 1) * 128], pb.ap[0:64, 0:128],
                       [pb], [bkht_hi])
                v_stt(kht_lo[0:64, h * 128:(h + 1) * 128], pb.ap[0:64, 0:128], 1.0,
                      kht_hi[0:64, h * 128:(h + 1) * 128], AL.mult, AL.subtract,
                      [pb, bkht_hi], [bkht_lo])

            qtv = qt_hi.rearrange("p (h t) -> p h t", h=H)
            qlv = qt_lo.rearrange("p (h t) -> p h t", h=H)
            ktv = kt_hi.rearrange("p (h t) -> p h t", h=H)
            klv = kt_lo.rearrange("p (h t) -> p h t", h=H)
            vhv = v_hi.rearrange("p (h t) -> p h t", h=H)
            vlv = v_lo.rearrange("p (h t) -> p h t", h=H)

            for grp in range(2):
                pbs = [bank[2 * grp], bank[2 * grp + 1]]
                for hh in range(8):
                    h = grp * 8 + hh
                    pb = pbs[hh // 4]
                    oc = (hh % 4) * 128
                    mmb(pb, pb.ap[:, oc:oc + 128], ktv[:, h, :], qtv[:, h, :],
                        [bkt_hi, bqt_hi])
                    mmb(pb, pb.ap[:, oc:oc + 128], ktv[:, h, :], qlv[:, h, :],
                        [bkt_hi, bqt_lo])
                    mmb(pb, pb.ap[:, oc:oc + 128], klv[:, h, :], qtv[:, h, :],
                        [bkt_lo, bqt_hi])
                for half in range(2):
                    pb = pbs[half]
                    h0 = grp * 8 + half * 4
                    mbc = maskbd[:, :].unsqueeze(1).broadcast_to([128, 4, 128])
                    v_tt(pmf.rearrange("p (h t) -> p h t", h=H)[:, h0:h0 + 4, :],
                         pb.ap[:, :].rearrange("p (h t) -> p h t", h=4), mbc, AL.mult,
                         [pb, bmask], [bpmf])
            v_copy(pm_hi[:, :], pmf[:, :], [bpmf], [bpm_hi])
            v_stt(pm_lo[:, :], pmf[:, :], 1.0, pm_hi[:, :], AL.mult, AL.subtract,
                  [bpmf, bpm_hi], [bpm_lo])

            ybanks = [bank[4], bank[5], bank[6], bank[7]]
            pmhv = pm_hi.rearrange("p (h t) -> p h t", h=H)
            pmlv = pm_lo.rearrange("p (h t) -> p h t", h=H)
            for h in range(H):
                yb = ybanks[h // 4]
                oc = (h % 4) * 128
                mmb(yb, yb.ap[:, oc:oc + 128], pmhv[:, h, :], vhv[:, h, :],
                    [bpm_hi, bv_hi])
                mmb(yb, yb.ap[:, oc:oc + 128], pmhv[:, h, :], vlv[:, h, :],
                    [bpm_hi, bv_lo])
                mmb(yb, yb.ap[:, oc:oc + 128], pmlv[:, h, :], vhv[:, h, :],
                    [bpm_lo, bv_hi])

            v_copy(ybuf[:, :], pB[:, :], [bank[4], bank[5], bank[6], bank[7]], [bybuf])
            khtv_h = kht_hi.rearrange("p (h t) -> p h t", h=H)
            khtv_l = kht_lo.rearrange("p (h t) -> p h t", h=H)
            stv_h = st_hi.rearrange("p (h t) -> p h t", h=H)
            stv_l = st_lo.rearrange("p (h t) -> p h t", h=H)
            stf3 = stfv.rearrange("p (h t) -> p h t", h=H)
            ubanks = [bank[0], bank[1], bank[2], bank[3]]
            y2banks = [bank[4], bank[5], bank[6], bank[7]]
            for cc in range(2):
                e5 = smalls[:, 48:64]
                s_act(e5, nb31[:, cc * 16:(cc + 1) * 16], AF.Exp, [bnb31], [bsm],
                      scale=-1.0)
                e5bc = e5.unsqueeze(2).broadcast_to([128, H, 128])
                v_tt(stf3, sv, e5bc, AL.mult, [bS, bsm], [bstf])
                v_copy(st_hi[:, :], stfv[:, :], [bstf], [bst_hi])
                v_stt(st_lo[:, :], stfv[:, :], 1.0, st_hi[:, :], AL.mult, AL.subtract,
                      [bstf, bst_hi], [bst_lo])
                for h in range(H):
                    yb = y2banks[h // 4]
                    oc = (h % 4) * 128
                    qh_ = qtv[:, h, cc * 64:cc * 64 + 64]
                    ql_ = qlv[:, h, cc * 64:cc * 64 + 64]
                    orows = yb.ap[0:64, oc:oc + 128]
                    mmb(yb, orows, qh_, stv_h[:, h, :], [bqt_hi, bst_hi], half="lo")
                    mmb(yb, orows, qh_, stv_l[:, h, :], [bqt_hi, bst_lo], half="lo")
                    mmb(yb, orows, ql_, stv_h[:, h, :], [bqt_lo, bst_hi], half="lo")
                if cc == 0:
                    v_tt(ybuf[0:64, :], ybuf[0:64, :], pB[0:64, :], AL.add,
                         [bank[4], bank[5], bank[6], bank[7], bybuf], [bybuf])
                else:
                    v_copy(stfv[0:64, :], pB[0:64, :],
                           [bank[4], bank[5], bank[6], bank[7]], [bstf])
                    dma("gpsimd", stfv[64:128, :], stfv[0:64, :], [bstf], [bstf])
                    v_tt(ybuf[64:128, :], ybuf[64:128, :], stfv[64:128, :], AL.add,
                         [bstf, bybuf], [bybuf])
                if cc == 0:
                    for h in range(H):
                        ub = ubanks[h // 4]
                        oc = (h % 4) * 128
                        mmb(ub, ub.ap[:, oc:oc + 128], khtv_h[0:64, h, :],
                            vhv[0:64, h, :], [bkht_hi, bv_hi])
                        mmb(ub, ub.ap[:, oc:oc + 128], khtv_h[0:64, h, :],
                            vlv[0:64, h, :], [bkht_hi, bv_lo])
                        mmb(ub, ub.ap[:, oc:oc + 128], khtv_l[0:64, h, :],
                            vhv[0:64, h, :], [bkht_lo, bv_hi])
                else:
                    dma("gpsimd", qt_hi[0:64, :], v_hi[64:128, :], [bv_hi], [bqt_hi])
                    dma("gpsimd", qt_lo[0:64, :], v_lo[64:128, :], [bv_lo], [bqt_lo])
                    for h in range(H):
                        pb = bank[h % 2]
                        transp(khv[:, h, 64:128], pb.ap[0:64, 0:128], pb, [bkhf])
                        v_copy(kht_hi[0:64, h * 128:(h + 1) * 128], pb.ap[0:64, 0:128],
                               [pb], [bkht_hi])
                        v_stt(kht_lo[0:64, h * 128:(h + 1) * 128], pb.ap[0:64, 0:128],
                              1.0, kht_hi[0:64, h * 128:(h + 1) * 128],
                              AL.mult, AL.subtract, [pb, bkht_hi], [bkht_lo])
                    vh2 = qt_hi.rearrange("p (h t) -> p h t", h=H)
                    vl2 = qt_lo.rearrange("p (h t) -> p h t", h=H)
                    for h in range(H):
                        ub = ubanks[h // 4]
                        oc = (h % 4) * 128
                        mmb(ub, ub.ap[:, oc:oc + 128], khtv_h[0:64, h, :],
                            vh2[0:64, h, :], [bkht_hi, bqt_hi])
                        mmb(ub, ub.ap[:, oc:oc + 128], khtv_h[0:64, h, :],
                            vl2[0:64, h, :], [bkht_hi, bqt_lo])
                        mmb(ub, ub.ap[:, oc:oc + 128], khtv_l[0:64, h, :],
                            vh2[0:64, h, :], [bkht_lo, bqt_hi])
                e3 = smalls[:, 0:16]
                s_act(e3, nb63[:, cc * 16:(cc + 1) * 16], AF.Exp, [bnb63], [bsm],
                      scale=-1.0)
                for h in range(H):
                    ub = ubanks[h // 4]
                    oc = (h % 4) * 128
                    v_stt(sv[:, h, :], sv[:, h, :], e3[:, h:h + 1], ub.ap[:, oc:oc + 128],
                          AL.mult, AL.add, [bsm, ub], [bS])
            dma("scalar", y_sp[pair * 128:(pair + 1) * 128, :], ybuf[:, :], [bybuf], [bysp])
            drip_quant(quant_per_tile // 4)

        drip_quant(quant_per_tile - 4 * (quant_per_tile // 4))

    while qstate["i"] < n_quant:
        emit_quant_tile(qstate["i"])
        qstate["i"] += 1

    dv = dlog.rearrange("p (j h) -> p h j", j=32)
    v_red(rly_d[:, :], dv, AL.add, [bdlog], [brly_d])
    if not skip_relay:
        dma("gpsimd", relay_in[:, 0:D], Sst[:, :], [bS], [brin])
        dma("gpsimd", relay_in[:, D:D + 16], rly_d[:, :], [brly_d], [brin])

        def cc_fn(eng):
            return eng.collective_compute(
                "AllGather", AL.bypass,
                replica_groups=[[0, 1, 2, 3], [4, 5, 6, 7]],
                ins=[relay_in[:, :].opt()],
                outs=[relay_out[:, :].opt()])
        emit("gpsimd", cc_fn, [brin], [brout], inc=1, sem="cc")
        v_memset(sin_b[:, :], 0.0, [bsin])
        v_memset(fsum[:, :], 0.0, [bfsum])
        for jj in (2, 1, 0):
            dma("gpsimd", rly_t[:, :], relay_out[jj * 128:(jj + 1) * 128, 0:D],
                [brout], [brly_t])
            dma("gpsimd", rly_d[:, :], relay_out[jj * 128:(jj + 1) * 128, D:D + 16],
                [brout], [brly_d])
            s_act(fexp[:, :], fsum[:, :], AF.Exp, [bfsum], [bfexp])
            fbc = fexp[:, :].unsqueeze(2).broadcast_to([128, H, 128])
            v_tt(stf3, rly_t.rearrange("p (h t) -> p h t", h=H), fbc, AL.mult,
                 [brly_t, bfexp], [bstf])
            v_stt(sin_b[:, :], stfv[:, :], cmaskb[:, jj:jj + 1], sin_b[:, :],
                  AL.mult, AL.add, [bstf, bcm], [bsin])
            v_stt(fsum[:, :], rly_d[:, :], cmaskb[:, jj:jj + 1], fsum[:, :],
                  AL.mult, AL.add, [brly_d, bcm], [bfsum])
    else:
        v_memset(sin_b[:, :], 0.0, [bsin])
    for sgi in range(2):
        chap, chb = c0_hi_seg[sgi]
        clap, clb = c0_lo_seg[sgi]
        sl = slice(sgi * 1024, (sgi + 1) * 1024)
        v_copy(chap, sin_b[:, sl], [bsin], [chb])
        v_stt(clap, sin_b[:, sl], 1.0, chap, AL.mult, AL.subtract, [bsin, chb], [clb])
    v_memset(gcur[:, :], 0.0, [bgcur])

    for t in range(n_epi_tiles):
        tok0 = t * ET
        for tb in range(4):
            ap_, b_ = yt_seg[tb]
            dma("scalar", ap_, y_sp[tok0 + tb * 128:tok0 + tb * 128 + 128, :],
                [bysp], [b_])
        for pp in range(4):
            pair = t * 4 + pp
            dma("gpsimd", qt_hi[:, :], qth_sp[pair, :, :], [bqth], [bqt_hi])
            dma("gpsimd", qt_lo[:, :], qtl_sp[pair, :, :], [bqtl], [bqt_lo])
            for cc in range(2):
                gj = pair * 2 + cc
                v_tt(egam[:, cc * 16:(cc + 1) * 16], betlog[:, gj * 16:(gj + 1) * 16],
                     gcur[:, :], AL.add, [bbetlog, bgcur], [begam])
                s_act(egam[:, cc * 16:(cc + 1) * 16], egam[:, cc * 16:(cc + 1) * 16],
                      AF.Exp, [begam], [begam])
                v_tt(gcur[:, :], gcur[:, :], dlog[:, gj * 16:(gj + 1) * 16], AL.add,
                     [bgcur, bdlog], [bgcur])
            egv = egam.rearrange("p (c h) -> p c h", c=2).transpose([0, 2, 1])
            egbc = egv.unsqueeze(3).broadcast_to([128, H, 2, 64])
            v_tt(qtg_hi.rearrange("p (h c t) -> p h c t", h=H, c=2),
                 qt_hi.rearrange("p (h c t) -> p h c t", h=H, c=2), egbc, AL.mult,
                 [bqt_hi, begam], [bqtg_hi])
            v_tt(qtg_lo.rearrange("p (h c t) -> p h c t", h=H, c=2),
                 qt_lo.rearrange("p (h c t) -> p h c t", h=H, c=2), egbc, AL.mult,
                 [bqt_lo, begam], [bqtg_lo])
            dbanks = [bank[0], bank[1], bank[2], bank[3]]
            qgv = qtg_hi.rearrange("p (h t) -> p h t", h=H)
            qgl = qtg_lo.rearrange("p (h t) -> p h t", h=H)
            def c0h_f(h):
                ap_, b_ = c0_hi_seg[h // 8]
                return ap_[:, (h % 8) * 128:(h % 8 + 1) * 128], b_
            def c0l_f(h):
                ap_, b_ = c0_lo_seg[h // 8]
                return ap_[:, (h % 8) * 128:(h % 8 + 1) * 128], b_
            ap_, b_ = yt_seg[pp]
            for cc in range(2):
                r0 = cc * 64
                for h in range(H):
                    db = dbanks[h // 4]
                    oc = (h % 4) * 128
                    orows = db.ap[0:64, oc:oc + 128]
                    chap, chb = c0h_f(h)
                    clap, clb = c0l_f(h)
                    mmb(db, orows, qgv[:, h, r0:r0 + 64], chap, [bqtg_hi, chb], half="lo")
                    mmb(db, orows, qgv[:, h, r0:r0 + 64], clap, [bqtg_hi, clb], half="lo")
                    mmb(db, orows, qgl[:, h, r0:r0 + 64], chap, [bqtg_lo, chb], half="lo")
                if cc == 0:
                    v_tt(ap_[0:64, :], ap_[0:64, :], pA[0:64, :], AL.add,
                         [bank[0], bank[1], bank[2], bank[3], b_], [b_])
                else:
                    v_copy(ysc[0:64, :], pA[0:64, :],
                           [bank[0], bank[1], bank[2], bank[3]], [bysc])
                    dma("gpsimd", ysc[64:128, :], ysc[0:64, :], [bysc], [bysc])
                    v_tt(ap_[64:128, :], ap_[64:128, :], ysc[64:128, :], AL.add,
                         [bysc, b_], [b_])

        dma("sync", gamb[:, :], gam_in[:, :], [bconst], [bgam])
        dma("sync", betb[:, :], bet_in[:, :], [bconst], [bbet])
        for tb in range(4):
            yap, yb_ = yt_seg[tb]
            ln_stats_apply(yb_, yap, bysc, ysc[:, :], EPS_LN)
            v_tt(ysc[:, :], ysc[:, :], gamb[:, :], AL.mult, [bgam, bysc], [bysc])
            v_tt(ysc[:, :], ysc[:, :], betb[:, :], AL.add, [bbet, bysc], [bysc])
            for dc in range(16):
                pb = bank[6 + (dc % 2)]
                transp(ysc[:, dc * 128:(dc + 1) * 128], pb.ap[:, 0:128], pb, [bysc])
                col = dc * ET + tb * 128
                hseg, hb_ = seg(ynTh_seg, col, 128)
                v_copy(hseg, pb.ap[:, 0:128], [pb], [hb_])
                lseg, lb_ = seg(ynTl_seg, col, 128)
                v_stt(lseg, pb.ap[:, 0:128], 1.0, hseg, AL.mult, AL.subtract,
                      [pb, hb_], [lb_])

        for tbp in range(2):
            for dc in range(16):
                wh, bwh = (wrA_hi, bwrA_hi) if dc % 2 == 0 else (wrB_hi, bwrB_hi)
                wl, bwl = (wrA_lo, bwrA_lo) if dc % 2 == 0 else (wrB_lo, bwrB_lo)
                dma(wq_next(), wh[:, :], wsp_hi[2, dc * 128:(dc + 1) * 128, :], [bwsp], [bwh])
                dma(wq_next(), wl[:, :], wsp_lo[2, dc * 128:(dc + 1) * 128, :], [bwsp], [bwl])
                for ti in range(2):
                    tb = tbp * 2 + ti
                    col = dc * ET + tb * 128
                    yh, yhb = seg(ynTh_seg, col, 128)
                    yl, ylb = seg(ynTl_seg, col, 128)
                    for nt in range(4):
                        pb = bank[ti * 4 + nt]
                        mmb(pb, pb.ap[:, :], yh, wh[:, nt * 512:(nt + 1) * 512],
                            [yhb, bwh])
                        mmb(pb, pb.ap[:, :], yh, wl[:, nt * 512:(nt + 1) * 512],
                            [yhb, bwl])
                        mmb(pb, pb.ap[:, :], yl, wh[:, nt * 512:(nt + 1) * 512],
                            [ylb, bwh])
            for ti in range(2):
                tb = tbp * 2 + ti
                xap, xb_ = x1_seg[tb]
                dma("scalar", xap, x_in[tok0 + tb * 128:tok0 + tb * 128 + 128, :],
                    [bconst], [xb_])
                psrc = pA if ti == 0 else pB
                pbl = [bank[ti * 4 + i] for i in range(4)]
                v_tt(xap, xap, psrc[:, :], AL.add, pbl + [xb_], [xb_])

        for tb in range(4):
            xap, xb_ = x1_seg[tb]
            ln_stats_apply(xb_, xap, bysc, ysc[:, :], EPS_LN)
            for dc in range(16):
                pb = bank[6 + (dc % 2)]
                transp(ysc[:, dc * 128:(dc + 1) * 128], pb.ap[:, 0:128], pb, [bysc])
                col = dc * ET + tb * 128
                hseg, hb_ = seg(x2Th_seg, col, 128)
                v_copy(hseg, pb.ap[:, 0:128], [pb], [hb_])
                lseg, lb_ = seg(x2Tl_seg, col, 128)
                v_stt(lseg, pb.ap[:, 0:128], 1.0, hseg, AL.mult, AL.subtract,
                      [pb, hb_], [lb_])

        sqb = bank[5]
        for sc_ in range(4):
            for i16 in range(16):
                hb = sc_ * 16 + i16
                r = i16 % 2
                dma(wq_next(), fwg[r][:, :].rearrange("p (dc c) -> p dc c", dc=16),
                    wgu_eff[hb, :, :, :], [bwgue], [bfwg[r]])
                dma(wq_next(), fwu[r][:, :].rearrange("p (dc c) -> p dc c", dc=16),
                    wgu_eff[64 + hb, :, :, :], [bwgue], [bfwu[r]])
                gb_, ub_ = bank[6], bank[7]
                for dc in range(16):
                    xh, xhb = seg(x2Th_seg, dc * ET, 512)
                    xl, xlb = seg(x2Tl_seg, dc * ET, 512)
                    wg_ = fwg[r][:, dc * 128:(dc + 1) * 128]
                    wu_ = fwu[r][:, dc * 128:(dc + 1) * 128]
                    mmb(gb_, gb_.ap[:, :], wg_, xh, [bfwg[r], xhb])
                    mmb(gb_, gb_.ap[:, :], wg_, xl, [bfwg[r], xlb])
                    mmb(ub_, ub_.ap[:, :], wu_, xh, [bfwu[r], xhb])
                    mmb(ub_, ub_.ap[:, :], wu_, xl, [bfwu[r], xlb])
                s_act(hfb[:, :], gb_.ap[:, :], AF.Silu, [gb_], [bhf])
                v_tt(hfb[:, :], hfb[:, :], ub_.ap[:, :], AL.mult, [ub_, bhf], [bhf])
                hcol = i16 * 512
                hseg, hhb = seg(ynTh_seg, hcol, 512)
                v_copy(hseg, hfb[:, :], [bhf], [hhb])
                lseg, hlb = seg(ynTl_seg, hcol, 512)
                v_stt(lseg, hfb[:, :], 1.0, hseg, AL.mult, AL.subtract,
                      [bhf, hhb], [hlb])
                v_tt(h2b[:, :], hfb[:, :], hfb[:, :], AL.mult, [bhf], [bh2])
                mmb(sqb, sqb.ap[0:1, :], onesb[:, :], h2b[:, :], [bones, bh2], half="lo")
            for dt_ in range(4):
                for gidx in range(2):
                    dma(WQ, fwp[gidx][:, :].rearrange("p (hb c) -> p hb c", hb=4),
                        wp_eff[sc_ * 16 + gidx * 4:sc_ * 16 + (gidx + 1) * 4, dt_, :, :]
                        .transpose([1, 0, 2]), [bwpe], [bfwp[gidx]])
                for i4 in range(4):
                    rr = i4 % 2
                    if i4 >= 2:
                        dma(WQ, fwp[rr][:, :].rearrange("p (hb c) -> p hb c", hb=4),
                            wp_eff[sc_ * 16 + i4 * 4:sc_ * 16 + (i4 + 1) * 4, dt_, :, :]
                            .transpose([1, 0, 2]), [bwpe], [bfwp[rr]])
                    for ii in range(4):
                        i16 = i4 * 4 + ii
                        wp_ = fwp[rr][:, ii * 512:(ii + 1) * 512]
                        for tb in range(4):
                            ob = bank[tb]
                            hh_, hhb = seg(ynTh_seg, i16 * 512 + tb * 128, 128)
                            hl_, hlb = seg(ynTl_seg, i16 * 512 + tb * 128, 128)
                            mmb(ob, ob.ap[:, :], hh_, wp_, [hhb, bfwp[rr]])
                            mmb(ob, ob.ap[:, :], hl_, wp_, [hlb, bfwp[rr]])
                for tb in range(4):
                    ob = bank[tb]
                    yap, yb_ = yt_seg[tb]
                    dst = yap[:, dt_ * 512:(dt_ + 1) * 512]
                    if sc_ == 0:
                        v_copy(dst, ob.ap[:, :], [ob], [yb_])
                    else:
                        v_tt(dst, dst, ob.ap[:, :], AL.add, [ob, yb_], [yb_])
        v_copy(sqs[0:1, :], sqb.ap[0:1, :], [sqb], [bsqs])
        for tb in range(4):
            pb = bank[4]
            transp(sqs[0:1, tb * 128:(tb + 1) * 128], pb.ap[0:128, 0:1], pb, [bsqs])
            s_act(rinvt[:, tb:tb + 1], pb.ap[0:128, 0:1], AF.Sqrt, [pb], [brinv],
                  bias=float(EPS_RMS), scale=1.0 / HDIM)
        v_recip(rinvt[:, 4:8], rinvt[:, 0:4], [brinv], [brinv])
        for tb in range(4):
            yap, yb_ = yt_seg[tb]
            xap, xb_ = x1_seg[tb]
            v_stt(outb[:, :], yap, rinvt[:, 4 + tb:5 + tb], xap, AL.mult, AL.add,
                  [yb_, brinv, xb_], [boutb])
            dma("scalar", out_hbm[tok0 + tb * 128:tok0 + tb * 128 + 128, :],
                outb[:, :], [boutb], [bout])

    fin = [(snm, PROG.val[snm]) for snm in SEMNAMES if PROG.val[snm] > 0]

    def nop_fn(eng):
        return eng.engine_nop()
    PROG.op("gpsimd", nop_fn, fin, 1)

    with nc.Block() as block:
        sems = {}
        semstack = contextlib.ExitStack()
        for snm in SEMNAMES:
            sems[snm] = semstack.enter_context(nc.semaphore("sem_" + snm))

        def runner(eng_obj, ename):
            for fn, waits, inc, snm in PROG.q[ename]:
                for oe, ov in waits.items():
                    eng_obj.wait_ge(sems[oe], ov)
                inst = fn(eng_obj)
                inst.then_inc(sems[snm], inc)

        @block.tensor
        def _(eng):
            runner(eng, "tensor")

        @block.vector
        def _(eng):
            runner(eng, "vector")

        @block.scalar
        def _(eng):
            runner(eng, "scalar")

        @block.sync
        def _(eng):
            runner(eng, "sync")

        @block.gpsimd
        def _(eng):
            runner(eng, "gpsimd")

    stack.close()
    return nc


def _host_consts():
    bf = ml_dtypes.bfloat16
    ident = np.eye(128, dtype=np.float32)
    s_ = np.arange(64)
    u1 = (s_[:, None] <= s_[None, :]).astype(np.float32)
    U = np.zeros((128, 128), np.float32)
    U[0:64, 0:64] = u1
    U[64:128, 64:128] = u1
    M = U.copy()
    ag2 = np.zeros((128, 2), np.float32)
    ag2[0:64, 0] = 1.0
    ag2[64:128, 1] = 1.0
    agb = np.ascontiguousarray(ag2.T)
    ones = np.ones((128, 1), np.float32)
    return dict(identf=ident,
                ucum=np.ascontiguousarray(U.astype(bf)),
                maskbd=np.ascontiguousarray(M),
                ag2=np.ascontiguousarray(ag2.astype(bf)),
                agb=agb,
                onesb=np.ascontiguousarray(ones.astype(bf)))


def kernel(L0_attn_x, L0_attn_Wq, L0_attn_Wk, L0_attn_Wv, L0_attn_Wg, L0_attn_Wo,
           L0_attn_gamma, L0_attn_beta, L0_ffn_w_gate_up, L0_ffn_w_proj):
    x = np.ascontiguousarray(np.asarray(L0_attn_x, dtype=np.float32))
    ins_common = dict(
        wqT=np.ascontiguousarray(np.asarray(L0_attn_Wq, np.float32).T),
        wkT=np.ascontiguousarray(np.asarray(L0_attn_Wk, np.float32).T),
        wvT=np.ascontiguousarray(np.asarray(L0_attn_Wv, np.float32).T),
        wgT=np.ascontiguousarray(np.asarray(L0_attn_Wg, np.float32).T),
        woT=np.ascontiguousarray(np.asarray(L0_attn_Wo, np.float32).T),
        wguT=np.ascontiguousarray(np.asarray(L0_ffn_w_gate_up, np.float32).T),
        wpT=np.ascontiguousarray(np.asarray(L0_ffn_w_proj, np.float32).T),
        gammab=np.ascontiguousarray(
            np.broadcast_to(np.asarray(L0_attn_gamma, np.float32), (128, D))),
        betab=np.ascontiguousarray(
            np.broadcast_to(np.asarray(L0_attn_beta, np.float32), (128, D))),
    )
    ins_common.update(_host_consts())

    nc = build()
    in_maps = []
    for core in range(8):
        b, p = core // 4, core % 4
        cm = np.zeros((128, 4), np.float32)
        cm[:, 0:p] = 1.0
        m = dict(ins_common)
        m["x"] = np.ascontiguousarray(x[b, p * TPC:(p + 1) * TPC])
        m["cmask"] = cm
        in_maps.append(m)
    res = run_bass_kernel_spmd(nc, in_maps, core_ids=list(range(8)))
    out = np.zeros((2, 8192, D), np.float32)
    for core in range(8):
        b, p = core // 4, core % 4
        out[b, p * TPC:(p + 1) * TPC] = res.results[core]["out"]
    return out
